# version 09
# baseline (speedup 1.0000x reference)
import contextlib
import numpy as np
import ml_dtypes
import concourse.bass as bass
import concourse.mybir as mybir
from concourse.bass_utils import run_bass_kernel_spmd

F32 = mybir.dt.float32
BF16 = mybir.dt.bfloat16
AF = mybir.ActivationFunctionType
ALU = mybir.AluOpType
AX = mybir.AxisListType

T = 2048
TS = 256
NSUB = T // TS
D = 1024
DI = 2048
NC_D = D // 128
NC_DI = DI // 128
NST = 16
RNK = 64
FF = 4096
NH = 4
SFULL = 8192
ENGS = ("sync", "tensor", "vector", "scalar", "gpsimd")
RG = [[0, 1, 2, 3], [4, 5, 6, 7]]


def bld(cut=None):
    nc = bass.Bass("TRN2", target_bir_lowering=False, debug=False)

    def din(name, shape, dt=F32):
        return nc.dram_tensor(name, shape, dt, kind="ExternalInput").ap()

    x_own = din("x_own", [T, D])
    x_halo = din("x_halo", [3, D])
    wip = din("wip", [D, 2 * DI])
    convw = din("convw", [DI, 4])
    convb = din("convb", [DI])
    xpw = din("xpw", [DI, 96])
    dtw = din("dtw", [RNK, DI])
    dtb = din("dtb", [DI])
    dtln = din("dtln", [RNK])
    bln = din("bln", [NST])
    cln = din("cln", [NST])
    alog = din("alog", [DI, NST])
    dskip = din("dskip", [DI])
    opw = din("opw", [DI, D])
    ilnw = din("ilnw", [D])
    plnw = din("plnw", [D])
    gw = din("gw", [D, FF])
    uw = din("uw", [D, FF])
    dwn = din("dwn", [FF, D])
    qkvw = din("qkvw", [D, 768])
    qkvb = din("qkvb", [12, 64])
    wgate = din("wgate", [D, 8])
    bgate = din("bgate", [1, 8])
    wexp = din("wexp", [8, D, D])
    bexp = din("bexp", [8, D])
    idf = din("idf", [128, 128])
    selone = din("selone", [NST, NST * 128])
    cwb = din("cwb", [128, 16])
    cmb = din("cmb", [128, 4])
    plc = din("plc", [256, D])

    out_t = nc.dram_tensor("out", [T, D], F32, kind="ExternalOutput").ap()

    def dint(name, shape, dt=F32):
        return nc.dram_tensor(name, shape, dt, kind="Internal").ap()

    dxnf = dint("dxnf", [D, T])
    dxnb = dint("dxnb", [D, T], BF16)
    dxng = dint("dxng", [4 * D, T], BF16)
    dc2i = dint("dc2i", [DI, 17])
    dc2o = dint("dc2o", [4 * DI, 17])
    dati = dint("dati", [SFULL, D], BF16)
    dato = dint("dato", [T, D], BF16)
    dcum = dint("dcum", [DI, T])
    dyp = dint("dyp", [DI, T])
    dzs = dint("dzs", [DI, T], BF16)
    dhid = dint("dhid", [D, T])
    dmoe = dint("dmoe", [T, D])
    dexp = dint("dexp", [8, D, D], BF16)
    dgw2 = dint("dgw2", [128, NC_D * FF], BF16)
    duw2 = dint("duw2", [128, NC_D * FF], BF16)
    ddw2 = dint("ddw2", [128, (FF // 128) * D], BF16)
    dsm = dint("dsm", [T, D])
    dq = dint("dq", [64, NH * SFULL], BF16)

    q = {e: [] for e in ENGS}
    sem_names = (["s_pe", "s_dve", "s_act", "s_pool", "cc1", "cc2", "cc3"]
                 + [f"q_{e}_{i}" for e in ("sync", "scalar", "gpsimd")
                    for i in range(8)])
    dma_rr = {"sync": 0, "scalar": 0, "gpsimd": 0}
    cnt = {s: 0 for s in sem_names}
    esem = {"tensor": "s_pe", "vector": "s_dve", "scalar": "s_act",
            "gpsimd": "s_pool", "sync": "s_ld"}
    writer = {}
    readers = {}
    snaps = {}

    def snap(name):
        snaps[name] = {e: len(q[e]) for e in ENGS}
    lastwait = {e: {} for e in ENGS}
    SEM = {}

    def _wait(eng, sem, val):
        if val <= 0 or lastwait[eng].get(sem, 0) >= val:
            return
        lastwait[eng][sem] = val
        q[eng].append(lambda E, sem=sem, val=val: E[eng].wait_ge(SEM[sem], val))

    def emit(eng, fn, reads=(), writes=(), dma=False, cc=None):
        deps = []
        for k in reads:
            w = writer.get(k)
            if w:
                deps.append(w)
        for k in writes:
            w = writer.get(k)
            if w:
                deps.append(w)
            deps.extend(readers.get(k, ()))
        for (peng, sem, val, isasync) in deps:
            _wait(eng, sem, val)
        if cc is not None:
            sem = cc
            cnt[sem] = 1
            inc = None
            isasync = True
        elif dma:
            slot = dma_rr[eng]
            dma_rr[eng] = (slot + 1) % 8
            sem = f"q_{eng}_{slot}"
            if cnt[sem] > 0:
                _wait(eng, sem, cnt[sem])
            cnt[sem] += 16
            inc = 16
            isasync = True
        else:
            sem = esem[eng]
            cnt[sem] += 1
            inc = 1
            isasync = False
        mark = (eng, sem, cnt[sem], isasync)
        q[eng].append(lambda E, fn=fn, sem=sem, inc=inc:
                      (fn(E[eng]).then_inc(SEM[sem], inc) if inc is not None
                       else fn(E[eng]).then_inc(SEM[sem])))
        for k in reads:
            readers.setdefault(k, []).append(mark)
        for k in writes:
            writer[k] = mark
            readers[k] = []
        return mark

    def barrier():
        vals = dict(cnt)
        for e in ENGS:
            for s, v in vals.items():
                _wait(e, s, v)

    def mm(pkey, psl, lhsT, rhs, rkeys, start, stop):
        emit("tensor", lambda E: E.matmul(psl, lhsT, rhs, start=start,
                                          stop=stop, skip_group_check=True),
             reads=rkeys, writes=(pkey,))

    def transp(pkey, psl, insl, ident, rkeys):
        emit("tensor", lambda E: E.transpose(psl, insl, ident), reads=rkeys,
             writes=(pkey,))

    def load(dst_sl, src_ap, key, reads=(), eng="sync", slow=False):
        if slow:
            emit(eng, lambda E: E.dma_start(out=dst_sl, in_=src_ap,
                                            allow_slow_non_contiguous=True),
                 reads=reads, writes=(key,), dma=True)
        else:
            emit(eng, lambda E: E.dma_start(out=dst_sl, in_=src_ap),
                 reads=reads, writes=(key,), dma=True)

    def store(dst_ap, src_sl, rkeys, wkey, eng="scalar", slow=False):
        if slow:
            emit(eng, lambda E: E.dma_start(out=dst_ap, in_=src_sl,
                                            allow_slow_non_contiguous=True),
                 reads=rkeys, writes=(wkey,), dma=True)
        else:
            emit(eng, lambda E: E.dma_start(out=dst_ap, in_=src_sl),
                 reads=rkeys, writes=(wkey,), dma=True)

    es = contextlib.ExitStack()
    with es:
        def sb(st_, name, shape, dt=F32):
            return st_.enter_context(nc.sbuf_tensor(name, shape, dt))

        for s in sem_names:
            SEM[s] = es.enter_context(nc.semaphore(s))

        csb = sb(es, "csb", [128, 10])
        ones = sb(es, "ones", [128, TS])
        id_f = sb(es, "id_f", [128, 128])
        id_b = sb(es, "id_b", [128, 128], BF16)
        sel1 = sb(es, "sel1", [NST, NST * 128], BF16)
        onesb = sb(es, "onesb", [128, 1], BF16)
        crms = sb(es, "crms", [NST, T], BF16)
        hcar = sb(es, "hcar", [128, NC_DI * NST])
        ccar = sb(es, "ccar", [128, NC_DI])
        aneg = sb(es, "aneg", [128, NC_DI * NST])
        dcolb = sb(es, "dcolb", [128, NC_DI])
        cbcol = sb(es, "cbcol", [128, NC_DI])
        dtbcol = sb(es, "dtbcol", [128, NC_DI])
        wcv = sb(es, "wcv", [128, NC_DI * 4])
        blncol = sb(es, "blncol", [NST, 1])
        clncol = sb(es, "clncol", [NST, 1])
        dlncol = sb(es, "dlncol", [RNK, 1])
        cwbs = sb(es, "cwbs", [128, 16])
        cmbs = sb(es, "cmbs", [128, 4])
        Hbuf = sb(es, "Hbuf", [128, NC_DI * NST])
        v16 = sb(es, "v16", [16, 128])
        vr1 = sb(es, "vr1", [1, 128])

        CONSTS = [0.0, 1.0, 0.125, -1.0, 1e-6, 1e-5, 1.0 / 16, 1.0 / 64,
                  1.0 / 1024]
        for i, v in enumerate(CONSTS):
            nc.const_aps.register(F32, float(v), csb[:, i : i + 1])

        PSW = es.enter_context(nc.psum_tensor("psw", [128, 2048], F32))
        PS = [PSW[:, i * 512:(i + 1) * 512] for i in range(4)]
        PS.append(es.enter_context(nc.psum_tensor("pf4", [128, 512], F32)))
        PS.append(es.enter_context(nc.psum_tensor("pf5", [128, 512], F32)))
        PB = [es.enter_context(nc.psum_tensor(f"pb{i}", [128, 128], BF16))
              for i in range(2)]
        PK = [f"PS{i}" for i in range(6)]
        PBK = [f"PB{i}" for i in range(2)]

        for i, v in enumerate(CONSTS):
            emit("vector", lambda E, i=i, v=v: E.memset(csb[:, i:i + 1],
                                                        float(v)),
                 writes=(f"c{i}",))
        emit("vector", lambda E: E.memset(ones[:, :], 1.0), writes=("ones",))
        emit("vector", lambda E: E.memset(onesb[:, :], 1.0), writes=("onesb",))
        emit("vector", lambda E: E.memset(hcar[:, :], 0.0),
             writes=tuple(f"hcar{c}_{n}" for c in range(NC_DI)
                          for n in range(NST)))
        emit("vector", lambda E: E.memset(ccar[:, :], 0.0),
             writes=tuple(f"ccar{c}" for c in range(NC_DI)))
        _wait("scalar", "s_dve", cnt["s_dve"])
        _wait("gpsimd", "s_dve", cnt["s_dve"])

        load(id_f[:, :], idf, "idf")
        emit("vector", lambda E: E.tensor_copy(id_b[:, :], id_f[:, :]),
             reads=("idf",), writes=("idb",))
        load(cwbs[:, :], cwb, "cwbs")
        load(cmbs[:, :], cmb, "cmbs")

        def vec_cols(dram_ap, dstbuf, key, nchunk):
            load(v16[0:nchunk, :], dram_ap.rearrange("(c p) -> c p", p=128),
                 "v16")
            transp(PK[5], PS[5][0:128, 0:nchunk], v16[0:nchunk, :],
                   id_f[0:nchunk, 0:nchunk], ("v16", "idf"))
            emit("vector", lambda E: E.tensor_copy(dstbuf[:, 0:nchunk],
                                                   PS[5][:, 0:nchunk]),
                 reads=(PK[5],), writes=(key, "v16_done"))

        def row_col(dram_ap, n, dstbuf, key):
            load(vr1[0:1, 0:n], dram_ap.rearrange("(o n) -> o n", o=1), "vr1")
            transp(PK[5], PS[5][0:n, 0:1], vr1[0:1, 0:n], id_f[0:1, 0:1],
                   ("vr1", "idf"))
            emit("vector", lambda E: E.tensor_copy(dstbuf[0:n, 0:1],
                                                   PS[5][0:n, 0:1]),
                 reads=(PK[5],), writes=(key, "vr1_done"))

        vec_cols(convb, cbcol, "cbcol", NC_DI)
        vec_cols(dtb, dtbcol, "dtbcol", NC_DI)
        vec_cols(dskip, dcolb, "dcolb", NC_DI)
        row_col(bln, NST, blncol, "blncol")
        row_col(cln, NST, clncol, "clncol")
        row_col(dtln, RNK, dlncol, "dlncol")

        load(wcv[:, :].rearrange("p (c k) -> p c k", c=NC_DI),
             convw.rearrange("(c p) k -> p c k", p=128), "wcv")
        load(aneg[:, :].rearrange("p (c n) -> p c n", c=NC_DI),
             alog.rearrange("(c p) n -> p c n", p=128), "aneg_raw")
        emit("scalar", lambda E: E.activation(aneg[:, :], aneg[:, :], AF.Exp),
             reads=("aneg_raw",), writes=("aneg_e",))
        emit("vector", lambda E: E.tensor_scalar(aneg[:, :], aneg[:, :], -1.0,
                                                 None, ALU.mult),
             reads=("aneg_e",), writes=("aneg",))

        a1 = contextlib.ExitStack()
        xtok = [sb(a1, "xtok0", [128, D])] * 2
        sqt = sb(a1, "sqt", [128, D])
        xnt = sb(a1, "xnt", [128, D])
        xrt = sb(a1, "xrt", [128, D], BF16)
        stat = sb(a1, "stat", [128, 8])
        xnfs = sb(a1, "xnfs", [128, NC_D * TS])
        xnbt = sb(a1, "xnbt", [128, 128], BF16)
        xrfm = sb(a1, "xrfm", [128, NC_D * TS], BF16)
        hfm = sb(a1, "hfm", [128, NC_D * 4], BF16)
        wipb = sb(a1, "wipb", [128, NC_D * 2 * DI], BF16)
        wstg = [sb(a1, f"wstg{i}", [128, 1024]) for i in range(2)]
        ilncol = sb(a1, "ilncol", [128, NC_D])
        xpwb = sb(a1, "xpwb", [128, NC_DI * 96], BF16)
        dtwb = sb(a1, "dtwb", [RNK, DI], BF16)
        xc = sb(a1, "xc", [128, NC_DI * (TS + 3)], BF16)
        x1 = sb(a1, "x1", [128, NC_DI * TS], BF16)
        zst = [sb(a1, f"zst{i}", [128, TS], BF16) for i in range(2)]
        dts = sb(a1, "dts", [128, NC_DI * TS])
        dtxb = sb(a1, "dtxb", [128, TS])
        cumt = [sb(a1, f"cumt{i}", [128, TS]) for i in range(2)]
        Bb = sb(a1, "Bb", [128, NST * TS], BF16)
        Cb = sb(a1, "Cb", [128, NST * TS], BF16)
        at_ = [sb(a1, f"at{i}", [128, TS]) for i in range(2)]
        bt_ = [sb(a1, f"bt{i}", [128, TS]) for i in range(2)]
        ht16 = sb(a1, "ht16", [128, NST * TS])
        yc_ = sb(a1, "yc", [128, TS])
        hct = sb(a1, "hct", [128, TS])
        ypt = [sb(a1, f"ypt{i}", [128, TS]) for i in range(2)]
        u_ = sb(a1, "u_", [128, TS])
        u2_ = sb(a1, "u2_", [128, TS])
        u3_ = sb(a1, "u3_", [128, TS])
        dtr = sb(a1, "dtr", [RNK, TS])
        dtn = sb(a1, "dtn", [RNK, TS], BF16)
        br_ = sb(a1, "br_", [NST, TS])
        cr_ = sb(a1, "cr_", [NST, TS])
        brb = sb(a1, "brb", [NST, TS], BF16)
        rr_ = sb(a1, "rr_", [1, TS])
        cacc = sb(a1, "cacc", [128, TS])

        vec_cols(ilnw, ilncol, "ilncol", NC_D)
        for sh in range(2):
            load(wstg[sh][0:16, 0:1024], selone[:, sh * 1024:(sh + 1) * 1024],
                 f"wst{sh}")
            emit("vector", lambda E, sh=sh: E.tensor_copy(
                sel1[:, sh * 1024:(sh + 1) * 1024], wstg[sh][0:16, 0:1024]),
                reads=(f"wst{sh}",), writes=("sel1",))

        for c in range(NC_D):
            for hq in range(4):
                w_i = hq % 2
                load(wstg[w_i][:, :1024],
                     wip[c * 128:(c + 1) * 128,
                         hq * 1024:(hq + 1) * 1024], f"wst{w_i}")
                emit("vector", lambda E, c=c, hq=hq, w_i=w_i:
                     E.tensor_scalar(
                         wipb[:, c * 4096 + hq * 1024:
                              c * 4096 + (hq + 1) * 1024],
                         wstg[w_i][:, :1024], ilncol[:, c:c + 1], None,
                         ALU.mult),
                     reads=(f"wst{w_i}", "ilncol"), writes=(f"wipb{c}",))
        for c in range(NC_DI):
            load(wstg[c % 2][:, 0:96], xpw[c * 128:(c + 1) * 128, :],
                 f"wst{c % 2}")
            emit("vector", lambda E, c=c: E.tensor_copy(
                xpwb[:, c * 96:(c + 1) * 96], wstg[c % 2][:, 0:96]),
                reads=(f"wst{c % 2}",), writes=(f"xpwb{c}",))
        for sh in range(2):
            load(wstg[sh][0:64, 0:1024], dtw[:, sh * 1024:(sh + 1) * 1024],
                 f"wst{sh}")
            emit("vector", lambda E, sh=sh: E.tensor_scalar(
                dtwb[:, sh * 1024:(sh + 1) * 1024], wstg[sh][0:64, 0:1024],
                dlncol[:, 0:1], None, ALU.mult),
                reads=(f"wst{sh}", "dlncol"), writes=("dtwb",))

        snap("w")
        load(xtok[0][0:3, :], x_halo, "xt0")

        def ln_rows(nrow, src, srckeys):
            emit("scalar", lambda E: E.activation(sqt[0:nrow, :], src,
                                                  AF.Square),
                 reads=srckeys, writes=("sqt",))
            emit("vector", lambda E: E.reduce_sum(stat[0:nrow, 0:1], src,
                                                  axis=AX.X),
                 reads=srckeys, writes=("st0",))
            emit("vector", lambda E: E.reduce_sum(stat[0:nrow, 1:2],
                                                  sqt[0:nrow, :], axis=AX.X),
                 reads=("sqt",), writes=("st1",))
            emit("vector", lambda E: E.tensor_scalar(stat[0:nrow, 2:3],
                 stat[0:nrow, 0:1], 1.0 / D, None, ALU.mult),
                 reads=("st0",), writes=("stm",))
            emit("vector", lambda E: E.tensor_tensor(stat[0:nrow, 3:4],
                 stat[0:nrow, 2:3], stat[0:nrow, 2:3], ALU.mult),
                 reads=("stm",), writes=("stm2",))
            emit("vector", lambda E: E.tensor_scalar(stat[0:nrow, 4:5],
                 stat[0:nrow, 1:2], 1.0 / D, None, ALU.mult),
                 reads=("st1",), writes=("stv0",))
            emit("vector", lambda E: E.tensor_tensor(stat[0:nrow, 4:5],
                 stat[0:nrow, 4:5], stat[0:nrow, 3:4], ALU.subtract),
                 reads=("stv0", "stm2"), writes=("stv",))
            emit("scalar", lambda E: E.activation(stat[0:nrow, 5:6],
                 stat[0:nrow, 4:5], AF.Sqrt, bias=1e-5),
                 reads=("stv",), writes=("stsd",))
            emit("vector", lambda E: E.reciprocal(stat[0:nrow, 5:6],
                 stat[0:nrow, 5:6]),
                 reads=("stsd",), writes=("strd",))
            emit("vector", lambda E: E.tensor_scalar(xnt[0:nrow, :], src,
                 stat[0:nrow, 2:3], stat[0:nrow, 5:6], ALU.subtract, ALU.mult),
                 reads=srckeys + ("stm", "strd"), writes=("xnt",))
            emit("vector", lambda E: E.tensor_tensor(stat[0:nrow, 6:7],
                 stat[0:nrow, 5:6], stat[0:nrow, 5:6], ALU.mult),
                 reads=("strd",), writes=("sr2",))
            emit("vector", lambda E: E.tensor_tensor(stat[0:nrow, 6:7],
                 stat[0:nrow, 6:7], stat[0:nrow, 4:5], ALU.mult),
                 reads=("sr2", "stv"), writes=("svr",))
            emit("scalar", lambda E: E.activation(stat[0:nrow, 7:8],
                 stat[0:nrow, 6:7], AF.Sqrt, bias=1e-6),
                 reads=("svr",), writes=("ssd2",))
            emit("vector", lambda E: E.reciprocal(stat[0:nrow, 7:8],
                 stat[0:nrow, 7:8]),
                 reads=("ssd2",), writes=("srd2",))
            emit("vector", lambda E: E.tensor_scalar(xrt[0:nrow, :],
                 xnt[0:nrow, :], stat[0:nrow, 7:8], None, ALU.mult),
                 reads=("xnt", "srd2"), writes=("xrt",))

        ln_rows(3, xtok[0][0:3, :], ("xt0",))
        for c in range(NC_D):
            transp(PBK[0], PB[0][0:128, 0:3], xrt[0:3, c * 128:(c + 1) * 128],
                   id_b[0:3, 0:3], ("xrt", "idb"))
            emit("vector", lambda E, c=c: E.tensor_copy(
                hfm[:, c * 4:c * 4 + 3], PB[0][:, 0:3]),
                reads=(PBK[0],), writes=(f"hfm{c}",))
        for m in range(NC_DI):
            for c in range(NC_D):
                mm(PK[0], PS[0][0:128, 0:3],
                   wipb[:, c * 4096 + m * 128: c * 4096 + (m + 1) * 128],
                   hfm[:, c * 4:c * 4 + 3],
                   (f"wipb{c}", f"hfm{c}"), c == 0, c == NC_D - 1)
            emit("vector", lambda E, m=m: E.tensor_copy(
                xc[:, m * (TS + 3): m * (TS + 3) + 3], PS[0][:, 0:3]),
                reads=(PK[0],), writes=(f"xcc{m}",))

        snap("halo")
        for s in range(NSUB):
            for k in range(2):
                tt = s * 2 + k
                load(xtok[0][:, :], x_own[tt * 128:(tt + 1) * 128, :],
                     "xt0")
                ln_rows(128, xtok[0][:, :], ("xt0",))
                for c in range(NC_D):
                    transp(PK[4], PS[4][0:128, 0:128],
                           xnt[:, c * 128:(c + 1) * 128], id_f[:, :],
                           ("xnt", "idf"))
                    if s == 0 and k == 0 and c == 0:
                        snap("t1")
                    emit("vector", lambda E, c=c, k=k: E.tensor_copy(
                        xnfs[:, c * TS + k * 128: c * TS + (k + 1) * 128],
                        PS[4][:, 0:128]), reads=(PK[4],), writes=(f"xnfs{c}",))
                    if s == 0 and k == 0 and c == 0:
                        snap("t2")
                    emit("vector", lambda E, c=c, k=k: E.tensor_copy(
                        xnbt[:, :], PS[4][:, 0:128]),
                        reads=(PK[4],), writes=("xnbt",))
                    store(dxnb[c * 128:(c + 1) * 128,
                               tt * 128:(tt + 1) * 128],
                          xnbt[:, :], ("xnbt",), "dxnb_st", eng="scalar")
                    if s == 0 and k == 0 and c == 0:
                        snap("t3")
                    transp(PBK[k], PB[k][0:128, 0:128],
                           xrt[:, c * 128:(c + 1) * 128], id_b[:, :],
                           ("xrt", "idb"))
                    if s == 0 and k == 0 and c == 0:
                        snap("t4")
                    emit("vector", lambda E, c=c, k=k: E.tensor_copy(
                        xrfm[:, c * TS + k * 128: c * TS + (k + 1) * 128],
                        PB[k][:, 0:128]), reads=(PBK[k],), writes=(f"xrfm{c}",))
                    if s == 0 and k == 0 and c == 0:
                        snap("t5")
                if s == 0 and k == 0:
                    snap("t6")
            snap(f"ln{s}")
            store(dxnf[:, s * TS:(s + 1) * TS].rearrange("(c p) f -> p c f",
                                                         p=128),
                  xnfs[:, :].rearrange("p (c f) -> p c f", c=NC_D),
                  tuple(f"xnfs{c}" for c in range(NC_D)), "dxnf_st")
            for m in range(2 * NC_DI):
                pi = m % 2
                for c in range(NC_D):
                    mm(PK[pi], PS[pi][0:128, 0:TS],
                       wipb[:, c * 4096 + m * 128: c * 4096 + (m + 1) * 128],
                       xrfm[:, c * TS:(c + 1) * TS],
                       (f"wipb{c}", f"xrfm{c}"), c == 0, c == NC_D - 1)
                if m < NC_DI:
                    emit("vector", lambda E, m=m, pi=pi: E.tensor_copy(
                        xc[:, m * (TS + 3) + 3: (m + 1) * (TS + 3)],
                        PS[pi][:, 0:TS]), reads=(PK[pi],), writes=(f"xcc{m}",))
                else:
                    zc = m - NC_DI
                    emit("scalar", lambda E, zc=zc, pi=pi: E.activation(
                        zst[zc % 2][:, :], PS[pi][:, 0:TS], AF.Silu),
                        reads=(PK[pi],), writes=(f"zst{zc % 2}",))
                    store(dzs[zc * 128:(zc + 1) * 128, s * TS:(s + 1) * TS],
                          zst[zc % 2][:, :], (f"zst{zc % 2}",), "dzs_st",
                          eng="scalar")
            snap(f"ip{s}")
            for c in range(NC_DI):
                base = c * (TS + 3)
                emit("vector", lambda E, c=c, base=base: E.tensor_scalar(
                    cacc[:, :], xc[:, base:base + TS],
                    wcv[:, c * 4:c * 4 + 1], None, ALU.mult),
                    reads=(f"xcc{c}", "wcv"), writes=("cacc",))
                for kk in range(1, 4):
                    emit("vector", lambda E, c=c, base=base, kk=kk:
                         E.scalar_tensor_tensor(cacc[:, :],
                             xc[:, base + kk:base + kk + TS],
                             wcv[:, c * 4 + kk:c * 4 + kk + 1],
                             cacc[:, :], ALU.mult, ALU.add),
                         reads=(f"xcc{c}", "wcv", "cacc"), writes=("cacc",))
                emit("scalar", lambda E, c=c: E.activation(
                    x1[:, c * TS:(c + 1) * TS], cacc[:, :], AF.Silu,
                    bias=cbcol[:, c:c + 1]),
                    reads=("cacc", "cbcol"), writes=(f"x1{c}",))
                emit("gpsimd", lambda E, base=base: E.tensor_copy(
                    xc[:, base:base + 3], xc[:, base + TS:base + TS + 3]),
                    reads=(f"xcc{c}",), writes=(f"xcc{c}",))
            snap(f"cv{s}")
            for c in range(NC_DI):
                st0, sp0 = (c == 0), (c == NC_DI - 1)
                mm(PK[2], PS[2][0:RNK, 0:TS], xpwb[:, c * 96:c * 96 + 64],
                   x1[:, c * TS:(c + 1) * TS], (f"xpwb{c}", f"x1{c}"), st0,
                   sp0)
                mm(PK[3], PS[3][0:NST, 0:TS], xpwb[:, c * 96 + 64:c * 96 + 80],
                   x1[:, c * TS:(c + 1) * TS], (f"xpwb{c}", f"x1{c}"), st0,
                   sp0)
                mm(PK[4], PS[4][0:NST, 0:TS], xpwb[:, c * 96 + 80:c * 96 + 96],
                   x1[:, c * TS:(c + 1) * TS], (f"xpwb{c}", f"x1{c}"), st0,
                   sp0)

            def rms_small(pidx, nrow, rawbuf, outsl, okey, lncol, lnkey,
                          scale):
                emit("vector", lambda E: E.tensor_copy(rawbuf[0:nrow, :],
                                                       PS[pidx][0:nrow, 0:TS]),
                     reads=(PK[pidx],), writes=(okey + "_raw",))
                emit("scalar", lambda E: E.activation(u3_[0:nrow, :],
                     rawbuf[0:nrow, :], AF.Square),
                     reads=(okey + "_raw",), writes=(okey + "_sq",))
                mm(PK[5], PS[5][0:1, 0:TS], ones[0:nrow, 0:1],
                   u3_[0:nrow, :], (okey + "_sq", "ones"), True, True)
                emit("scalar", lambda E, scale=scale: E.activation(
                    rr_[0:1, :], PS[5][0:1, 0:TS], AF.Sqrt, bias=1e-6,
                    scale=scale),
                    reads=(PK[5],), writes=(okey + "_sd",))
                emit("vector", lambda E: E.reciprocal(rr_[0:1, :],
                                                      rr_[0:1, :]),
                     reads=(okey + "_sd",), writes=(okey + "_rs",))
                mm(PK[5], PS[5][0:nrow, 0:TS], ones[0:1, 0:nrow],
                   rr_[0:1, :], (okey + "_rs", "ones"), True, True)
                if lncol is None:
                    emit("vector", lambda E: E.tensor_tensor(
                        outsl, rawbuf[0:nrow, :], PS[5][0:nrow, 0:TS],
                        ALU.mult),
                        reads=(okey + "_raw", PK[5]), writes=(okey,))
                else:
                    emit("vector", lambda E: E.tensor_tensor(
                        rawbuf[0:nrow, :], rawbuf[0:nrow, :],
                        PS[5][0:nrow, 0:TS], ALU.mult),
                        reads=(okey + "_raw", PK[5]), writes=(okey + "_n",))
                    emit("vector", lambda E: E.tensor_scalar(
                        outsl, rawbuf[0:nrow, :], lncol, None, ALU.mult),
                        reads=(okey + "_n", lnkey), writes=(okey,))

            rms_small(2, RNK, dtr, dtn[:, :], "dtn", None, None, 1.0 / 64)
            rms_small(3, NST, br_, brb[0:NST, :], "brms", blncol[:, 0:1],
                      "blncol", 1.0 / 16)
            rms_small(4, NST, cr_, crms[:, s * TS:(s + 1) * TS], "crms",
                      clncol[:, 0:1], "clncol", 1.0 / 16)
            snap(f"xp{s}")
            for n in range(NST):
                mm(PK[2], PS[2][0:128, 0:TS], sel1[:, n * 128:(n + 1) * 128],
                   brb[0:NST, :], ("brms", "sel1"), True, True)
                emit("scalar", lambda E, n=n: E.copy(
                    Bb[:, n * TS:(n + 1) * TS], PS[2][:, 0:TS]),
                    reads=(PK[2],), writes=(f"Bb{n}",))
                mm(PK[3], PS[3][0:128, 0:TS], sel1[:, n * 128:(n + 1) * 128],
                   crms[:, s * TS:(s + 1) * TS], ("crms", "sel1"), True, True)
                emit("vector", lambda E, n=n: E.tensor_copy(
                    Cb[:, n * TS:(n + 1) * TS], PS[3][:, 0:TS]),
                    reads=(PK[3],), writes=(f"Cb{n}",))
            snap(f"bc{s}")
            for c in range(NC_DI):
                pi = 2 + (c % 2)
                mm(PK[pi], PS[pi][0:128, 0:TS], dtwb[:, c * 128:(c + 1) * 128],
                   dtn[:, :], ("dtwb", "dtn"), True, True)
                emit("vector", lambda E, pi=pi, c=c: E.tensor_scalar(
                    u_[:, :], PS[pi][:, 0:TS], dtbcol[:, c:c + 1], None,
                    ALU.add),
                    reads=(PK[pi], "dtbcol"), writes=("spu",))
                emit("scalar", lambda E: E.activation(u2_[:, :], u_[:, :],
                                                      AF.Abs),
                     reads=("spu",), writes=("spa",))
                emit("scalar", lambda E: E.activation(u2_[:, :], u2_[:, :],
                                                      AF.Exp, scale=-1.0),
                     reads=("spa",), writes=("spe",))
                emit("scalar", lambda E: E.activation(u2_[:, :], u2_[:, :],
                                                      AF.Ln, bias=1.0),
                     reads=("spe",), writes=("spl",))
                emit("vector", lambda E: E.tensor_scalar(u3_[:, :], u_[:, :],
                                                         0.0, None, ALU.max),
                     reads=("spu",), writes=("spr",))
                emit("vector", lambda E, c=c: E.tensor_tensor(
                    dts[:, c * TS:(c + 1) * TS], u3_[:, :], u2_[:, :],
                    ALU.add),
                    reads=("spr", "spl"), writes=(f"dts{c}",))
            snap(f"dt{s}")
            for c in range(NC_DI):
                emit("vector", lambda E, c=c: E.tensor_tensor(
                    dtxb[:, :], dts[:, c * TS:(c + 1) * TS],
                    x1[:, c * TS:(c + 1) * TS], ALU.mult),
                    reads=(f"dts{c}", f"x1{c}"), writes=("dtx",))
                ci_ = c % 2
                emit("vector", lambda E, c=c, ci_=ci_: E.tensor_tensor_scan(
                    cumt[ci_][:, :], ones[:, 0:TS], dts[:, c * TS:(c + 1) * TS],
                    ccar[:, c:c + 1], ALU.mult, ALU.add),
                    reads=(f"dts{c}", "ones", f"ccar{c}"),
                    writes=(f"cumt{ci_}",))
                emit("vector", lambda E, c=c, ci_=ci_: E.tensor_copy(
                    ccar[:, c:c + 1], cumt[ci_][:, TS - 1:TS]),
                    reads=(f"cumt{ci_}",), writes=(f"ccar{c}",))
                store(dcum[c * 128:(c + 1) * 128, s * TS:(s + 1) * TS],
                      cumt[ci_][:, :], (f"cumt{ci_}",), "dcum_st")
                for n in range(NST):
                    ab = at_[n % 2]
                    bb = bt_[n % 2]
                    hsl = ht16[:, n * TS:(n + 1) * TS]
                    beng = "vector"
                    emit("scalar", lambda E, c=c, n=n, ab=ab: E.activation(
                        ab[:, :], dts[:, c * TS:(c + 1) * TS], AF.Exp,
                        scale=aneg[:, c * NST + n:c * NST + n + 1]),
                        reads=(f"dts{c}", "aneg"), writes=(f"at{n % 2}",))
                    emit(beng, lambda E, n=n, bb=bb: E.tensor_tensor(
                        bb[:, :], dtxb[:, :], Bb[:, n * TS:(n + 1) * TS],
                        ALU.mult),
                        reads=("dtx", f"Bb{n}"), writes=(f"bt{n % 2}",))
                    emit("vector", lambda E, c=c, n=n, ab=ab, bb=bb, hsl=hsl:
                         E.tensor_tensor_scan(hsl, ab[:, :], bb[:, :],
                             hcar[:, c * NST + n:c * NST + n + 1],
                             ALU.mult, ALU.add),
                         reads=(f"at{n % 2}", f"bt{n % 2}", f"hcar{c}_{n}"),
                         writes=(f"ht{n}",))
                    if n == 0:
                        emit("gpsimd", lambda E, n=n, hsl=hsl: E.tensor_tensor(
                            yc_[:, :], hsl, Cb[:, n * TS:(n + 1) * TS],
                            ALU.mult),
                            reads=(f"ht{n}", f"Cb{n}"), writes=("yc",))
                    else:
                        emit("gpsimd", lambda E, n=n, hsl=hsl: E.tensor_tensor(
                            hct[:, :], hsl, Cb[:, n * TS:(n + 1) * TS],
                            ALU.mult),
                            reads=(f"ht{n}", f"Cb{n}"), writes=("hctk",))
                        emit("gpsimd", lambda E: E.tensor_tensor(
                            yc_[:, :], yc_[:, :], hct[:, :], ALU.add),
                            reads=("yc", "hctk"), writes=("yc",))
                emit("vector", lambda E, c=c: E.tensor_copy(
                    hcar[:, c * NST:(c + 1) * NST],
                    ht16[:, TS - 1:NST * TS:TS]),
                    reads=tuple(f"ht{n}" for n in range(NST)),
                    writes=tuple(f"hcar{c}_{n}" for n in range(NST)))
                emit("vector", lambda E, c=c, ci_=ci_: E.scalar_tensor_tensor(
                    ypt[ci_][:, :], x1[:, c * TS:(c + 1) * TS],
                    dcolb[:, c:c + 1], yc_[:, :], ALU.mult, ALU.add),
                    reads=(f"x1{c}", "dcolb", "yc"), writes=(f"ypt{ci_}",))
                store(dyp[c * 128:(c + 1) * 128, s * TS:(s + 1) * TS],
                      ypt[ci_][:, :], (f"ypt{ci_}",), "dyp_st")

            snap(f"s{s}")
        HKEYS = tuple(f"hcar{c}_{n}" for c in range(NC_DI)
                      for n in range(NST))
        CKEYS = tuple(f"ccar{c}" for c in range(NC_DI))
        store(dc2i[:, 0:16].rearrange("(c p) n -> p c n", p=128),
              hcar[:, :].rearrange("p (c n) -> p c n", c=NC_DI),
              HKEYS, "dc2i_h", slow=True)
        store(dc2i[:, 16:17].rearrange("(c p) n -> p c n", p=128),
              ccar[:, :].rearrange("p (c o) -> p c o", o=1),
              CKEYS, "dc2i_c", slow=True)
        emit("gpsimd", lambda E: E.collective_compute(
            "AllGather", ALU.bypass, replica_groups=RG,
            ins=[dc2i.opt()], outs=[dc2o.opt()]),
            reads=("dc2i_h", "dc2i_c"), writes=("dc2o",), cc="cc2")
        emit("gpsimd", lambda E: E.collective_compute(
            "AllGather", ALU.bypass, replica_groups=RG,
            ins=[dxnb.opt()], outs=[dxng.opt()]),
            reads=("dxnb_st",), writes=("dxng",), cc="cc1")

        snap("cc")
        barrier()
        a1.close()

        hnst = contextlib.ExitStack()
        hn = sb(hnst, "hn", [128, NC_D * T], BF16)

        b1 = contextlib.ExitStack()
        hg = [sb(b1, f"hg{r}", [128, NC_DI * 17]) for r in range(4)]
        sih = sb(b1, "sih", [128, 2])
        et_ = sb(b1, "et_", [128, NST])
        opwb = sb(b1, "opwb", [128, NC_DI * D], BF16)
        cumb = [sb(b1, f"cumb{i}", [128, 512]) for i in range(2)]
        ypb = [sb(b1, f"ypb{i}", [128, 512]) for i in range(2)]
        yfb = sb(b1, "yfb", [128, NC_DI * 512], BF16)
        zsb = sb(b1, "zsb", [128, 512], BF16)
        cb2all = sb(b1, "cb2all", [128, NST * 512], BF16)
        ee_ = sb(b1, "ee_", [128, 512])
        tt_ = sb(b1, "tt_", [128, 512])
        xnfb = [sb(b1, f"xnfb{i}", [128, 512]) for i in range(2)]
        hidb = sb(b1, "hidb", [128, NC_D * 512])
        sqb = sb(b1, "sqb", [128, 512], BF16)
        rsr = sb(b1, "rsr", [1, 512])
        wst2 = [sb(b1, f"wst2{i}", [128, 1024]) for i in range(2)]
        plncol = sb(b1, "plncol", [128, NC_D])

        for r in range(4):
            load(hg[r][:, :].rearrange("p (c v) -> p c v", c=NC_DI),
                 dc2o[r * DI:(r + 1) * DI, :].rearrange("(c p) v -> p c v",
                                                        p=128),
                 f"hg{r}", reads=("dc2o",))
        for c in range(NC_DI):
            emit("vector", lambda E, c=c: E.memset(
                Hbuf[:, c * NST:(c + 1) * NST], 0.0), writes=(f"H{c}",))
            for r in range(4):
                emit("vector", lambda E, c=c, r=r: E.tensor_scalar(
                    sih[:, 0:1], hg[0][:, c * 17 + 16:c * 17 + 17],
                    cwbs[:, r * 4:r * 4 + 1], None, ALU.mult),
                    reads=("hg0", "cwbs"), writes=("sih",))
                for kk in range(1, 4):
                    emit("vector", lambda E, c=c, r=r, kk=kk:
                         E.scalar_tensor_tensor(sih[:, 0:1],
                             hg[kk][:, c * 17 + 16:c * 17 + 17],
                             cwbs[:, r * 4 + kk:r * 4 + kk + 1],
                             sih[:, 0:1], ALU.mult, ALU.add),
                         reads=(f"hg{kk}", "cwbs", "sih"), writes=("sih",))
                emit("scalar", lambda E, c=c: E.activation(
                    et_[:, :], aneg[:, c * NST:(c + 1) * NST], AF.Exp,
                    scale=sih[:, 0:1]),
                    reads=("aneg", "sih"), writes=("et",))
                emit("vector", lambda E, c=c, r=r: E.tensor_tensor(
                    et_[:, :], et_[:, :], hg[r][:, c * 17:c * 17 + 16],
                    ALU.mult),
                    reads=("et", f"hg{r}"), writes=("et2",))
                emit("vector", lambda E, c=c, r=r: E.scalar_tensor_tensor(
                    Hbuf[:, c * NST:(c + 1) * NST], et_[:, :],
                    cmbs[:, r:r + 1], Hbuf[:, c * NST:(c + 1) * NST],
                    ALU.mult, ALU.add),
                    reads=("et2", "cmbs", f"H{c}"), writes=(f"H{c}",))

        snap("carry")
        vec_cols(plnw, plncol, "plncol", NC_D)
        for c in range(NC_DI):
            load(wst2[c % 2][:, 0:1024], opw[c * 128:(c + 1) * 128, :],
                 f"w2{c % 2}")
            emit("vector", lambda E, c=c: E.tensor_copy(
                opwb[:, c * 1024:(c + 1) * 1024], wst2[c % 2][:, 0:1024]),
                reads=(f"w2{c % 2}",), writes=(f"opwb{c}",))

        for tb in range(4):
            for n in range(NST):
                mm(PK[3], PS[3][0:128, 0:512], sel1[:, n * 128:(n + 1) * 128],
                   crms[:, tb * 512:(tb + 1) * 512], ("crms", "sel1"), True,
                   True)
                emit("vector", lambda E, n=n: E.tensor_copy(
                    cb2all[:, n * 512:(n + 1) * 512], PS[3][:, 0:512]),
                    reads=(PK[3],), writes=(f"cb2_{n}",))
            for c in range(NC_DI):
                ci = c % 2
                load(cumb[ci][:, :],
                     dcum[c * 128:(c + 1) * 128, tb * 512:(tb + 1) * 512],
                     f"cumb{ci}", reads=("dcum_st",))
                load(ypb[ci][:, :],
                     dyp[c * 128:(c + 1) * 128, tb * 512:(tb + 1) * 512],
                     f"ypb{ci}", reads=("dyp_st",))
                for n in range(NST):
                    emit("scalar", lambda E, c=c, n=n, ci=ci: E.activation(
                        ee_[:, :], cumb[ci][:, :], AF.Exp,
                        scale=aneg[:, c * NST + n:c * NST + n + 1]),
                        reads=(f"cumb{ci}", "aneg"), writes=("ee",))
                    emit("vector", lambda E, c=c, n=n: E.scalar_tensor_tensor(
                        tt_[:, :], ee_[:, :],
                        Hbuf[:, c * NST + n:c * NST + n + 1],
                        cb2all[:, n * 512:(n + 1) * 512],
                        ALU.mult, ALU.mult),
                        reads=("ee", f"H{c}", f"cb2_{n}"), writes=("ttc",))
                    emit("gpsimd", lambda E, ci=ci: E.tensor_tensor(
                        ypb[ci][:, :], ypb[ci][:, :], tt_[:, :], ALU.add),
                        reads=("ttc", f"ypb{ci}"), writes=(f"ypb{ci}",))
                load(zsb[:, :], dzs[c * 128:(c + 1) * 128,
                                    tb * 512:(tb + 1) * 512], "zsb",
                     reads=("dzs_st",))
                emit("vector", lambda E, c=c, ci=ci: E.tensor_tensor(
                    yfb[:, c * 512:(c + 1) * 512], ypb[ci][:, :], zsb[:, :],
                    ALU.mult),
                    reads=(f"ypb{ci}", "zsb"), writes=(f"yfb{c}",))
            for mo in range(NC_D):
                pi = mo % 2
                for c in range(NC_DI):
                    mm(PK[pi], PS[pi][0:128, 0:512],
                       opwb[:, c * 1024 + mo * 128: c * 1024 + (mo + 1) * 128],
                       yfb[:, c * 512:(c + 1) * 512],
                       (f"opwb{c}", f"yfb{c}"), c == 0, c == NC_DI - 1)
                load(xnfb[pi][:, :], dxnf[mo * 128:(mo + 1) * 128,
                                          tb * 512:(tb + 1) * 512],
                     f"xnfb{pi}", reads=("dxnf_st",))
                emit("vector", lambda E, mo=mo, pi=pi: E.tensor_tensor(
                    hidb[:, mo * 512:(mo + 1) * 512], PS[pi][:, 0:512],
                    xnfb[pi][:, :], ALU.add),
                    reads=(PK[pi], f"xnfb{pi}"), writes=(f"hidb{mo}",))
                store(dhid[mo * 128:(mo + 1) * 128, tb * 512:(tb + 1) * 512],
                      hidb[:, mo * 512:(mo + 1) * 512], (f"hidb{mo}",),
                      "dhid_st")
                emit("scalar", lambda E, mo=mo: E.activation(
                    sqb[:, :], hidb[:, mo * 512:(mo + 1) * 512], AF.Square),
                    reads=(f"hidb{mo}",), writes=("sqb",))
                mm(PK[5], PS[5][0:1, 0:512], onesb[:, 0:1], sqb[:, :],
                   ("sqb", "onesb"), mo == 0, mo == NC_D - 1)
            emit("scalar", lambda E: E.activation(rsr[0:1, :],
                 PS[5][0:1, 0:512], AF.Sqrt, bias=1e-6, scale=1.0 / 1024),
                 reads=(PK[5],), writes=("rsd",))
            emit("vector", lambda E: E.reciprocal(rsr[0:1, :], rsr[0:1, :]),
                 reads=("rsd",), writes=("rsrk",))
            mm(PK[4], PS[4][0:128, 0:512], ones[0:1, 0:128], rsr[0:1, :],
               ("rsrk", "ones"), True, True)
            for mo in range(NC_D):
                emit("vector", lambda E, mo=mo, tb=tb: E.tensor_tensor(
                    hn[:, mo * T + tb * 512: mo * T + (tb + 1) * 512],
                    hidb[:, mo * 512:(mo + 1) * 512], PS[4][:, 0:512],
                    ALU.mult),
                    reads=(f"hidb{mo}", PK[4], "dhid_st"),
                    writes=(f"hn{mo}_{tb}",))

        snap("b1")
        barrier()
        b1.close()

        a2 = contextlib.ExitStack()
        wst2a = [sb(a2, f"wst2a{i}", [128, 1024]) for i in range(2)]
        plncol2 = sb(a2, "plncol2", [128, NC_D])
        wge = [sb(a2, f"wge{i}", [128, 4096], BF16) for i in range(2)]
        xts = a2.enter_context(nc.sbuf_tensor("xts", [128, NC_D * 128], BF16))
        wgs = a2.enter_context(nc.sbuf_tensor("wgs", [128, NC_D * 8], F32))
        xtf = a2.enter_context(nc.sbuf_tensor("xtf", [128, NC_D * 128], F32))
        bgb = sb(a2, "bgb", [128, 8])
        bgr = sb(a2, "bgr", [1, 8])
        glog = sb(a2, "glog", [128, 8])
        gstat = sb(a2, "gstat", [128, 6])
        gsel = sb(a2, "gsel", [128, 8])
        gtmp = sb(a2, "gtmp", [128, 8])
        comb = sb(a2, "comb", [128, 16 * 8])
        combT = sb(a2, "combT", [8, 16 * 128], BF16)
        bexps = sb(a2, "bexps", [8, D], BF16)
        webb = sb(a2, "webb", [128, NC_D * 1024], BF16)
        macc = sb(a2, "macc", [128, 4 * 1024])
        mev = sb(a2, "mev", [128, 512])

        vec_cols(plnw, plncol2, "plncol2", NC_D)

        for c in range(NC_D):
            for src_w, dst_w, skey in ((gw, dgw2, "dgw2_st"),
                                       (uw, duw2, "duw2_st")):
                wi = c % 2
                for hq in range(4):
                    load(wst2a[hq % 2][:, :], src_w[c * 128:(c + 1) * 128,
                         hq * 1024:(hq + 1) * 1024], f"w2a{hq % 2}")
                    emit("vector", lambda E, c=c, wi=wi, hq=hq:
                         E.tensor_scalar(wge[wi][:, hq * 1024:
                                               (hq + 1) * 1024],
                             wst2a[hq % 2][:, :], plncol2[:, c:c + 1], None,
                             ALU.mult),
                         reads=(f"w2a{hq % 2}", "plncol2"), writes=(f"wge{wi}",))
                store(dst_w[:, c * 4096:(c + 1) * 4096], wge[wi][:, :],
                      (f"wge{wi}",), skey)
        for c in range(FF // 128):
            load(wst2a[c % 2][:, 0:1024], dwn[c * 128:(c + 1) * 128, :],
                 f"w2a{c % 2}")
            emit("gpsimd", lambda E, c=c: E.tensor_copy(
                wge[c % 2][:, 0:1024], wst2a[c % 2][:, 0:1024]),
                reads=(f"w2a{c % 2}",), writes=(f"wge{c % 2}",))
            store(ddw2[:, c * 1024:(c + 1) * 1024], wge[c % 2][:, 0:1024],
                  (f"wge{c % 2}",), "ddw2_st")
        for e in range(8):
            for c in range(NC_D):
                load(wst2a[c % 2][:, 0:1024], wexp[e, c * 128:(c + 1) * 128, :],
                     f"w2a{c % 2}")
                emit("gpsimd", lambda E, c=c: E.tensor_copy(
                    wge[c % 2][:, 0:1024], wst2a[c % 2][:, 0:1024]),
                    reads=(f"w2a{c % 2}",), writes=(f"wge{c % 2}",))
                store(dexp[e, c * 128:(c + 1) * 128, :], wge[c % 2][:, 0:1024],
                      (f"wge{c % 2}",), "dexp_st")

        load(wst2a[0][:, 0:8 * NC_D].rearrange("p (c e) -> p c e", c=NC_D),
             wgate.rearrange("(c p) e -> p c e", p=128), "w2a0")
        emit("vector", lambda E: E.tensor_copy(wgs[:, :],
                                               wst2a[0][:, 0:8 * NC_D]),
             reads=("w2a0",), writes=("wgs",))
        load(bgr[0:1, :], bgate, "bgr")
        mm(PK[5], PS[5][0:128, 0:8], ones[0:1, 0:128], bgr[0:1, :],
           ("bgr", "ones"), True, True)
        emit("vector", lambda E: E.tensor_copy(bgb[:, :], PS[5][:, 0:8]),
             reads=(PK[5],), writes=("bgb",))
        load(wst2a[0][0:8, :], bexp.rearrange("(o e) f -> o e f", o=1)[0],
             "w2a0")
        emit("vector", lambda E: E.tensor_copy(bexps[:, :], wst2a[0][0:8, :]),
             reads=("w2a0",), writes=("bexps",))

        for tt in range(16):
            for c in range(NC_D):
                load(xtf[:, c * 128:(c + 1) * 128],
                     dxnf[c * 128:(c + 1) * 128, tt * 128:(tt + 1) * 128],
                     f"xtf{c}", reads=("dxnf_st",))
                mm(PK[2], PS[2][0:128, 0:8], xtf[:, c * 128:(c + 1) * 128],
                   wgs[:, c * 8:(c + 1) * 8], (f"xtf{c}", "wgs"), c == 0,
                   c == NC_D - 1)
            emit("vector", lambda E: E.tensor_tensor(glog[:, :], PS[2][:, 0:8],
                                                     bgb[:, :], ALU.add),
                 reads=(PK[2], "bgb"), writes=("glog",))
            emit("vector", lambda E: E.reduce_max(gstat[:, 0:1], glog[:, :],
                                                  axis=AX.X),
                 reads=("glog",), writes=("gm",))
            emit("vector", lambda E: E.tensor_scalar(gstat[:, 1:2],
                 gstat[:, 0:1], -1.0, None, ALU.mult),
                 reads=("gm",), writes=("gmn",))
            emit("scalar", lambda E: E.activation(glog[:, :], glog[:, :],
                 AF.Exp, bias=gstat[:, 1:2]),
                 reads=("glog", "gmn"), writes=("glog",))
            emit("vector", lambda E: E.reduce_sum(gstat[:, 2:3], glog[:, :],
                                                  axis=AX.X),
                 reads=("glog",), writes=("gz",))
            emit("vector", lambda E: E.reciprocal(gstat[:, 2:3],
                                                  gstat[:, 2:3]),
                 reads=("gz",), writes=("gzr",))
            emit("vector", lambda E: E.tensor_scalar(glog[:, :], glog[:, :],
                 gstat[:, 2:3], None, ALU.mult),
                 reads=("glog", "gzr"), writes=("gate_p",))
            emit("vector", lambda E: E.reduce_max(gstat[:, 3:4], glog[:, :],
                                                  axis=AX.X),
                 reads=("gate_p",), writes=("gm1",))
            emit("vector", lambda E: E.tensor_single_scalar(gsel[:, :],
                 glog[:, :], gstat[:, 3:4], ALU.is_ge),
                 reads=("gate_p", "gm1"), writes=("geq",))
            emit("vector", lambda E: E.tensor_scalar(gtmp[:, :], gsel[:, :],
                 gstat[:, 3:4], None, ALU.mult),
                 reads=("geq", "gm1"), writes=("geqm",))
            emit("vector", lambda E: E.tensor_tensor(gtmp[:, :], glog[:, :],
                 gtmp[:, :], ALU.subtract),
                 reads=("gate_p", "geqm"), writes=("gmask",))
            emit("vector", lambda E: E.reduce_max(gstat[:, 5:6], gtmp[:, :],
                                                  axis=AX.X),
                 reads=("gmask",), writes=("gm2",))
            emit("vector", lambda E: E.tensor_single_scalar(gsel[:, :],
                 glog[:, :], gstat[:, 5:6], ALU.is_ge),
                 reads=("gate_p", "gm2"), writes=("gsel",))
            emit("vector", lambda E, tt=tt: E.tensor_tensor(
                comb[:, tt * 8:(tt + 1) * 8], glog[:, :], gsel[:, :],
                ALU.mult),
                reads=("gate_p", "gsel"), writes=(f"comb{tt}",))
            transp(PK[3], PS[3][0:8, 0:128], comb[:, tt * 8:(tt + 1) * 8],
                   id_f[:, :], (f"comb{tt}", "idf"))
            emit("vector", lambda E, tt=tt: E.tensor_copy(
                combT[:, tt * 128:(tt + 1) * 128], PS[3][0:8, 0:128]),
                reads=(PK[3],), writes=(f"combT{tt}",))

        for g in range(4):
            for t4 in range(4):
                tt = g * 4 + t4
                for hh in range(2):
                    mm(PK[4], PS[4][0:128, 0:512],
                       combT[:, tt * 128:(tt + 1) * 128],
                       bexps[:, hh * 512:(hh + 1) * 512],
                       (f"combT{tt}", "bexps"), True, True)
                    emit("vector", lambda E, t4=t4, hh=hh: E.tensor_copy(
                        macc[:, t4 * 1024 + hh * 512: t4 * 1024 +
                             (hh + 1) * 512], PS[4][:, 0:512]),
                        reads=(PK[4],), writes=(f"macc{t4}",))
            for e in range(8):
                for c in range(NC_D):
                    load(webb[:, c * 1024:(c + 1) * 1024],
                         dexp[e, c * 128:(c + 1) * 128, :], "webb",
                         reads=("dexp_st",))
                for t4 in range(4):
                    tt = g * 4 + t4
                    for c in range(NC_D):
                        load(xts[:, c * 128:(c + 1) * 128],
                             dxnb[c * 128:(c + 1) * 128,
                                  tt * 128:(tt + 1) * 128], f"xts{c}",
                             reads=("dxnb_st",))
                    for hh in range(2):
                        pi = hh
                        for c in range(NC_D):
                            mm(PK[pi], PS[pi][0:128, 0:512],
                               xts[:, c * 128:(c + 1) * 128],
                               webb[:, c * 1024 + hh * 512:
                                    c * 1024 + (hh + 1) * 512],
                               (f"xts{c}", "webb"), c == 0,
                               c == NC_D - 1)
                        emit("vector", lambda E, pi=pi, e=e, tt=tt:
                             E.tensor_scalar(mev[:, :], PS[pi][:, 0:512],
                                 comb[:, tt * 8 + e: tt * 8 + e + 1], None,
                                 ALU.mult),
                             reads=(PK[pi], f"comb{tt}"), writes=("mev",))
                        emit("vector", lambda E, t4=t4, hh=hh:
                             E.tensor_tensor(
                                 macc[:, t4 * 1024 + hh * 512: t4 * 1024 +
                                      (hh + 1) * 512],
                                 macc[:, t4 * 1024 + hh * 512: t4 * 1024 +
                                      (hh + 1) * 512], mev[:, :], ALU.add),
                             reads=("mev", f"macc{t4}"), writes=(f"macc{t4}",))
            for t4 in range(4):
                tt = g * 4 + t4
                store(dmoe[tt * 128:(tt + 1) * 128, :],
                      macc[:, t4 * 1024:(t4 + 1) * 1024], (f"macc{t4}",),
                      "dmoe_st")

        snap("a2")
        barrier()
        a2.close()

        cs = contextlib.ExitStack()
        qw = sb(cs, "qw", [128, NC_D * 768], BF16)
        bqT = sb(cs, "bqT", [64, 12])
        bqs = sb(cs, "bqs", [12, 64])
        qst = [sb(cs, f"qst{i}", [64, 512], BF16) for i in range(2)]
        qld = [sb(cs, f"qld{i}", [64, 512], BF16) for i in range(2)]
        Kh = sb(cs, "Kh", [64, SFULL], BF16)
        Vf = sb(cs, "Vf", [64, SFULL], BF16)
        Vt = sb(cs, "Vt", [128, 64 * 65], BF16)
        xa = sb(cs, "xa", [128, NC_D * 512], BF16)
        Pt = [sb(cs, f"Pt{i}", [128, 2048], BF16) for i in range(2)]
        sO = [sb(cs, f"sO{i}", [65, 512]) for i in range(2)]
        rz_ = sb(cs, "rz_", [128, 1])
        aac = sb(cs, "aac", [128, 64 * 256], BF16)
        aaT = sb(cs, "aaT", [128, 256], BF16)
        plcs = sb(cs, "plcs", [128, 2 * D], BF16)
        afull = sb(cs, "afull", [128, D], BF16)
        wstc = [sb(cs, f"wstc{i}", [128, 768]) for i in range(2)]

        for c in range(NC_D):
            load(wstc[c % 2][:, :], qkvw[c * 128:(c + 1) * 128, :],
                 f"wc{c % 2}")
            emit("vector", lambda E, c=c: E.tensor_copy(
                qw[:, c * 768:(c + 1) * 768], wstc[c % 2][:, :]),
                reads=(f"wc{c % 2}",), writes=(f"qw{c}",))
        load(bqs[:, :], qkvb, "bqs_r")
        transp(PK[5], PS[5][0:64, 0:12], bqs[:, :], id_f[0:12, 0:12],
               ("bqs_r", "idf"))
        emit("vector", lambda E: E.tensor_copy(bqT[:, :], PS[5][0:64, 0:12]),
             reads=(PK[5],), writes=("bqT",))
        for c2 in range(2):
            load(wstc[0][:, 0:768], plc[c2 * 128:(c2 + 1) * 128, 0:768],
                 "wc0")
            load(wstc[1][:, 0:256], plc[c2 * 128:(c2 + 1) * 128, 768:1024],
                 "wc1")
            emit("vector", lambda E, c2=c2: E.tensor_copy(
                plcs[:, c2 * D:c2 * D + 768], wstc[0][:, 0:768]),
                reads=("wc0",), writes=("plcs",))
            emit("vector", lambda E, c2=c2: E.tensor_copy(
                plcs[:, c2 * D + 768:(c2 + 1) * D], wstc[1][:, 0:256]),
                reads=("wc1",), writes=("plcs",))

        for r in range(4):
            for tt in range(4):
                for c in range(NC_D):
                    load(xa[:, c * 512:(c + 1) * 512],
                         dxng[r * D + c * 128: r * D + (c + 1) * 128,
                              tt * 512:(tt + 1) * 512], f"xa{c}",
                         reads=("dxng",))
                for h in range(NH):
                    for c in range(NC_D):
                        mm(PK[2], PS[2][0:64, 0:512],
                           qw[:, c * 768 + h * 64: c * 768 + (h + 1) * 64],
                           xa[:, c * 512:(c + 1) * 512],
                           (f"qw{c}", f"xa{c}"), c == 0, c == NC_D - 1)
                    qi = (r * 4 + tt) % 2
                    emit("vector", lambda E, h=h, qi=qi: E.tensor_scalar(
                        qst[qi][:, :], PS[2][0:64, 0:512], bqT[:, h:h + 1],
                        None, ALU.add),
                        reads=(PK[2], "bqT"), writes=(f"qst{qi}",))
                    store(dq[:, h * SFULL + r * 2048 + tt * 512:
                             h * SFULL + r * 2048 + (tt + 1) * 512],
                          qst[qi][:, :], (f"qst{qi}",), "dq_st", eng="scalar")

        snap("qs")
        for h in range(NH):
            snap(f"h{h}")
            emit("vector", lambda E: E.memset(Vt[:, :], 1.0), writes=("Vt",))
            for r in range(4):
                for tt in range(4):
                    for c in range(NC_D):
                        load(xa[:, c * 512:(c + 1) * 512],
                             dxng[r * D + c * 128: r * D + (c + 1) * 128,
                                  tt * 512:(tt + 1) * 512], f"xa{c}",
                             reads=("dxng",))
                    for c in range(NC_D):
                        mm(PK[2], PS[2][0:64, 0:512],
                           qw[:, c * 768 + 256 + h * 64:
                              c * 768 + 256 + (h + 1) * 64],
                           xa[:, c * 512:(c + 1) * 512],
                           (f"qw{c}", f"xa{c}"), c == 0, c == NC_D - 1)
                    emit("vector", lambda E, h=h, r=r, tt=tt: E.tensor_scalar(
                        Kh[:, r * 2048 + tt * 512: r * 2048 + (tt + 1) * 512],
                        PS[2][0:64, 0:512], bqT[:, 4 + h:5 + h], None,
                        ALU.add),
                        reads=(PK[2], "bqT"), writes=("Kh",))
                    for c in range(NC_D):
                        mm(PK[3], PS[3][0:64, 0:512],
                           qw[:, c * 768 + 512 + h * 64:
                              c * 768 + 512 + (h + 1) * 64],
                           xa[:, c * 512:(c + 1) * 512],
                           (f"qw{c}", f"xa{c}"), c == 0, c == NC_D - 1)
                    emit("vector", lambda E, h=h, r=r, tt=tt: E.tensor_scalar(
                        Vf[:, r * 2048 + tt * 512: r * 2048 + (tt + 1) * 512],
                        PS[3][0:64, 0:512], bqT[:, 8 + h:9 + h], None,
                        ALU.add),
                        reads=(PK[3], "bqT"), writes=("Vf",))
            for kt in range(64):
                transp(PBK[kt % 2], PB[kt % 2][0:128, 0:64],
                       Vf[:, kt * 128:(kt + 1) * 128], id_b[0:64, 0:64],
                       ("Vf", "idb"))
                emit("vector", lambda E, kt=kt: E.tensor_copy(
                    Vt[:, kt * 65:kt * 65 + 64], PB[kt % 2][:, 0:64]),
                    reads=(PBK[kt % 2],), writes=("Vt",))
            for qc in range(16):
                po = 4 + (qc % 2)
                load(qld[qc % 2][:, :],
                     dq[:, h * SFULL + qc * 512: h * SFULL + (qc + 1) * 512],
                     f"qld{qc % 2}", reads=("dq_st",))
                for g_ in range(16):
                    gi = g_ % 2
                    for i4 in range(4):
                        mm(PK[i4], PS[i4][0:128, 0:512],
                           Kh[:, (g_ * 4 + i4) * 128:(g_ * 4 + i4 + 1) * 128],
                           qld[qc % 2][:, :],
                           ("Kh", f"qld{qc % 2}"), True, True)
                    emit("scalar", lambda E, gi=gi: E.activation(
                        Pt[gi][:, :], PSW[:, :], AF.Exp, scale=0.125),
                        reads=(PK[0], PK[1], PK[2], PK[3]),
                        writes=(f"Pt{gi}",))
                    for i4 in range(4):
                        kt = g_ * 4 + i4
                        mm(PK[po], PS[po][0:65, 0:512],
                           Vt[:, kt * 65:(kt + 1) * 65],
                           Pt[gi][:, i4 * 512:(i4 + 1) * 512],
                           ("Vt", f"Pt{gi}"), kt == 0, kt == 63)
                emit("vector", lambda E, po=po, qc=qc: E.tensor_copy(
                    sO[qc % 2][:, :], PS[po][0:65, 0:512]),
                    reads=(PK[po],), writes=(f"sO{qc % 2}",))
                for v in range(4):
                    transp(PK[2], PS[2][0:128, 0:65],
                           sO[qc % 2][:, v * 128:(v + 1) * 128],
                           id_f[0:65, 0:65], (f"sO{qc % 2}", "idf"))
                    emit("vector", lambda E: E.reciprocal(
                        rz_[:, :], PS[2][:, 64:65]),
                        reads=(PK[2],), writes=("rz",))
                    qb = qc * 4 + v
                    emit("vector", lambda E, qb=qb, h=h: E.tensor_scalar(
                        aac[:, qb * 256 + h * 64: qb * 256 + (h + 1) * 64],
                        PS[2][:, 0:64], rz_[:, :], None, ALU.mult),
                        reads=(PK[2], "rz"), writes=(f"aac{qb}_{h}",))

        snap("att")
        for qb in range(64):
            for ff in range(2):
                transp(PBK[ff], PB[ff][0:128, 0:128],
                       aac[:, qb * 256 + ff * 128: qb * 256 + (ff + 1) * 128],
                       id_b[:, :],
                       tuple(f"aac{qb}_{h}" for h in range(NH)) + ("idb",))
                emit("vector", lambda E, ff=ff: E.tensor_copy(
                    aaT[:, ff * 128:(ff + 1) * 128], PB[ff][:, 0:128]),
                    reads=(PBK[ff],), writes=("aaT",))
            for half in range(2):
                pi = half
                for ff in range(2):
                    mm(PK[pi], PS[pi][0:128, 0:512],
                       aaT[:, ff * 128:(ff + 1) * 128],
                       plcs[:, ff * D + half * 512: ff * D + (half + 1) * 512],
                       ("aaT", "plcs"), ff == 0, ff == 1)
                emit("vector", lambda E, pi=pi, half=half: E.tensor_copy(
                    afull[:, half * 512:(half + 1) * 512], PS[pi][:, 0:512]),
                    reads=(PK[pi],), writes=("afull",))
            store(dati[qb * 128:(qb + 1) * 128, :], afull[:, :], ("afull",),
                  "dati_st")
        emit("gpsimd", lambda E: E.collective_compute(
            "ReduceScatter", ALU.add, replica_groups=RG,
            ins=[dati.opt()], outs=[dato.opt()]),
            reads=("dati_st",), writes=("dato",), cc="cc3")

        snap("c")
        barrier()
        cs.close()

        b2 = contextlib.ExitStack()
        gm_ = [sb(b2, f"gm{i}", [128, NC_D * 128], BF16) for i in range(2)]
        um_ = [sb(b2, f"um{i}", [128, NC_D * 128], BF16) for i in range(2)]
        dm_ = [sb(b2, f"dm{i}", [128, 32 * 128], BF16) for i in range(2)]
        sg_ = sb(b2, "sg_", [128, 512], BF16)
        su_ = sb(b2, "su_", [128, 512], BF16)
        Vbig = sb(b2, "Vbig", [128, 32 * 512], BF16)
        hrl = [sb(b2, f"hrl{i}", [128, 512]) for i in range(2)]
        sst = sb(b2, "sst", [128, 512])
        stok = sb(b2, "stok", [128, 4 * 1024])

        for tb in range(4):
            for m in range(32):
                load(gm_[m % 2][:, :].rearrange("p (c f) -> p c f", c=NC_D),
                     dgw2.rearrange("p (c f) -> p c f", c=NC_D)[
                         :, :, m * 128:(m + 1) * 128],
                     f"gm{m % 2}", reads=("dgw2_st",))
                load(um_[m % 2][:, :].rearrange("p (c f) -> p c f", c=NC_D),
                     duw2.rearrange("p (c f) -> p c f", c=NC_D)[
                         :, :, m * 128:(m + 1) * 128],
                     f"um{m % 2}", reads=("duw2_st",))
                for c in range(NC_D):
                    mm(PK[0], PS[0][0:128, 0:512],
                       gm_[m % 2][:, c * 128:(c + 1) * 128],
                       hn[:, c * T + tb * 512: c * T + (tb + 1) * 512],
                       (f"gm{m % 2}", f"hn{c}_{tb}"), c == 0, c == NC_D - 1)
                emit("scalar", lambda E: E.activation(
                    sg_[:, :], PS[0][:, 0:512], AF.Silu),
                    reads=(PK[0],), writes=("sg",))
                for c in range(NC_D):
                    mm(PK[1], PS[1][0:128, 0:512],
                       um_[m % 2][:, c * 128:(c + 1) * 128],
                       hn[:, c * T + tb * 512: c * T + (tb + 1) * 512],
                       (f"um{m % 2}", f"hn{c}_{tb}"), c == 0, c == NC_D - 1)
                emit("vector", lambda E: E.tensor_copy(su_[:, :],
                                                       PS[1][:, 0:512]),
                     reads=(PK[1],), writes=("su",))
                emit("gpsimd", lambda E, m=m: E.tensor_tensor(
                    Vbig[:, m * 512:(m + 1) * 512], sg_[:, :], su_[:, :],
                    ALU.mult),
                    reads=("sg", "su"), writes=(f"Vbig{m}",))
            for ot in range(NC_D):
                load(dm_[ot % 2][:, :].rearrange("p (m f) -> p m f", m=32),
                     ddw2.rearrange("p (m f) -> p m f", m=32)[
                         :, :, ot * 128:(ot + 1) * 128],
                     f"dm{ot % 2}", reads=("ddw2_st",))
                pi = 4 + ot % 2
                for m in range(32):
                    mm(PK[pi], PS[pi][0:128, 0:512],
                       dm_[ot % 2][:, m * 128:(m + 1) * 128],
                       Vbig[:, m * 512:(m + 1) * 512],
                       (f"dm{ot % 2}", f"Vbig{m}"), m == 0, m == 31)
                load(hrl[ot % 2][:, :],
                     dhid[ot * 128:(ot + 1) * 128, tb * 512:(tb + 1) * 512],
                     f"hrl{ot % 2}", reads=("dhid_st",))
                emit("vector", lambda E, pi=pi, ot=ot: E.tensor_tensor(
                    sst[:, :], PS[pi][:, 0:512], hrl[ot % 2][:, :], ALU.add),
                    reads=(PK[pi], f"hrl{ot % 2}"), writes=("sst",))
                for v in range(4):
                    transp(PK[2], PS[2][0:128, 0:128],
                           sst[:, v * 128:(v + 1) * 128], id_f[:, :],
                           ("sst", "idf"))
                    emit("vector", lambda E, v=v, ot=ot: E.tensor_copy(
                        stok[:, v * 1024 + ot * 128:
                             v * 1024 + (ot + 1) * 128],
                        PS[2][:, 0:128]),
                        reads=(PK[2],), writes=(f"stok{v}_{ot}",))
            for v in range(4):
                tt = tb * 4 + v
                store(dsm[tt * 128:(tt + 1) * 128, :],
                      stok[:, v * 1024:(v + 1) * 1024],
                      tuple(f"stok{v}_{ot}" for ot in range(NC_D)), "dsm_st")

        snap("b2")
        barrier()
        b2.close()
        hnst.close()

        fs_ = contextlib.ExitStack()
        xf = [sb(fs_, f"xf{i}", [128, D]) for i in range(2)]
        sf = [sb(fs_, f"sf{i}", [128, D]) for i in range(2)]
        mf = [sb(fs_, f"mf{i}", [128, D]) for i in range(2)]
        af = [sb(fs_, f"af{i}", [128, D], BF16) for i in range(2)]
        gf = sb(fs_, "gf", [128, D])
        of = [sb(fs_, f"of{i}", [128, D]) for i in range(2)]

        for tt in range(16):
            i = tt % 2
            load(xf[i][:, :], x_own[tt * 128:(tt + 1) * 128, :], f"xf{i}")
            load(sf[i][:, :], dsm[tt * 128:(tt + 1) * 128, :], f"sf{i}",
                 reads=("dsm_st",))
            load(mf[i][:, :], dmoe[tt * 128:(tt + 1) * 128, :], f"mf{i}",
                 reads=("dmoe_st",))
            load(af[i][:, :], dato[tt * 128:(tt + 1) * 128, :], f"af{i}",
                 reads=("dato",))
            emit("vector", lambda E, i=i: E.tensor_tensor(
                gf[:, :], mf[i][:, :], af[i][:, :], ALU.add),
                reads=(f"mf{i}", f"af{i}"), writes=("gf",))
            emit("scalar", lambda E: E.activation(gf[:, :], gf[:, :], AF.Silu),
                 reads=("gf",), writes=("gf",))
            emit("vector", lambda E, i=i: E.tensor_tensor(
                gf[:, :], gf[:, :], sf[i][:, :], ALU.mult),
                reads=("gf", f"sf{i}"), writes=("gf",))
            emit("vector", lambda E, i=i: E.tensor_tensor(
                of[i][:, :], xf[i][:, :], gf[:, :], ALU.add),
                reads=("gf", f"xf{i}"), writes=(f"of{i}",))
            store(out_t[tt * 128:(tt + 1) * 128, :], of[i][:, :],
                  (f"of{i}",), f"out{i}")

        for i in range(8):
            _wait("sync", f"q_scalar_{i}", cnt[f"q_scalar_{i}"])
        fs_.close()

        if cut is not None:
            lim = snaps[cut]
            for e in ENGS:
                del q[e][lim[e]:]
        block = es.enter_context(nc.Block())
        Eh = {}

        @block.sync
        def _(sync):
            Eh["sync"] = sync
            for fn in q["sync"]:
                fn(Eh)

        @block.tensor
        def _(tensor):
            Eh["tensor"] = tensor
            for fn in q["tensor"]:
                fn(Eh)

        @block.vector
        def _(vector):
            Eh["vector"] = vector
            for fn in q["vector"]:
                fn(Eh)

        @block.scalar
        def _(scalar):
            Eh["scalar"] = scalar
            for fn in q["scalar"]:
                fn(Eh)

        @block.gpsimd
        def _(gpsimd):
            Eh["gpsimd"] = gpsimd
            for fn in q["gpsimd"]:
                fn(Eh)

    return nc


def kernel(**inp):
    x = inp["L0_attn_hidden_states"]
    B, S, _ = x.shape
    bf = ml_dtypes.bfloat16

    idf_h = np.eye(128, dtype=np.float32)
    sel_h = np.zeros((NST, NST * 128), dtype=np.float32)
    for n in range(NST):
        sel_h[n, n * 128:(n + 1) * 128] = 1.0

    wq = inp["L0_ffn_W_qkv"].reshape(D, 3, 16, 64)
    bq = inp["L0_ffn_b_qkv"].reshape(3, 16, 64)

    in_maps = []
    for c in range(8):
        b, j = c // 4, c % 4
        cwb_h = np.zeros((128, 16), dtype=np.float32)
        cmb_h = np.zeros((128, 4), dtype=np.float32)
        for r in range(4):
            if r < j:
                cmb_h[:, r] = 1.0
            for m_ in range(4):
                if r < m_ < j:
                    cwb_h[:, r * 4 + m_] = 1.0
        halo = (np.zeros((3, D), dtype=np.float32) if j == 0
                else np.ascontiguousarray(x[b, j * 2048 - 3: j * 2048]))
        plc_h = np.zeros((256, D), dtype=np.float32)
        for f in range(256):
            plc_h[f, j * 256 + f] = 1.0
        m = {
            "x_own": np.ascontiguousarray(x[b, j * 2048:(j + 1) * 2048]),
            "x_halo": halo,
            "wip": inp["L0_attn_in_proj_w"],
            "convw": inp["L0_attn_conv_w"],
            "convb": inp["L0_attn_conv_b"],
            "xpw": inp["L0_attn_x_proj_w"],
            "dtw": inp["L0_attn_dt_proj_w"],
            "dtb": inp["L0_attn_dt_proj_b"],
            "dtln": inp["L0_attn_dt_ln_w"],
            "bln": inp["L0_attn_b_ln_w"],
            "cln": inp["L0_attn_c_ln_w"],
            "alog": inp["L0_attn_A_log"],
            "dskip": inp["L0_attn_D_skip"],
            "opw": inp["L0_attn_out_proj_w"],
            "ilnw": inp["L0_attn_input_ln_w"],
            "plnw": inp["L0_attn_pre_moe_ln_w"],
            "gw": inp["L0_attn_gate_w"],
            "uw": inp["L0_attn_up_w"],
            "dwn": inp["L0_attn_down_w"],
            "qkvw": np.ascontiguousarray(
                wq[:, :, 4 * j:4 * j + 4, :].reshape(D, 768)),
            "qkvb": np.ascontiguousarray(
                bq[:, 4 * j:4 * j + 4, :].reshape(12, 64)),
            "wgate": inp["L0_ffn_W_gate"],
            "bgate": inp["L0_ffn_b_gate"].reshape(1, 8),
            "wexp": inp["L0_ffn_W_experts"],
            "bexp": inp["L0_ffn_b_experts"],
            "idf": idf_h,
            "selone": sel_h,
            "cwb": cwb_h,
            "cmb": cmb_h,
            "plc": plc_h,
        }
        in_maps.append(m)

    import os
    nc = bld(cut=os.environ.get("KCUT") or None)
    res = run_bass_kernel_spmd(nc, in_maps, core_ids=list(range(8)))
    out = np.empty((B, S, D), dtype=np.float32)
    for c in range(8):
        b, j = c // 4, c % 4
        out[b, j * 2048:(j + 1) * 2048] = res.results[c]["out"]
    return out




# version 10 (final)
# speedup vs baseline: 1.0059x; 1.0059x over previous
import contextlib
import numpy as np
import ml_dtypes
import concourse.bass as bass
import concourse.mybir as mybir
from concourse.bass_utils import run_bass_kernel_spmd

F32 = mybir.dt.float32
BF16 = mybir.dt.bfloat16
AF = mybir.ActivationFunctionType
ALU = mybir.AluOpType
AX = mybir.AxisListType

T = 2048
TS = 256
NSUB = T // TS
D = 1024
DI = 2048
NC_D = D // 128
NC_DI = DI // 128
NST = 16
RNK = 64
FF = 4096
NH = 4
SFULL = 8192
ENGS = ("sync", "tensor", "vector", "scalar", "gpsimd")
RG = [[0, 1, 2, 3], [4, 5, 6, 7]]


def bld(cut=None):
    nc = bass.Bass("TRN2", target_bir_lowering=False, debug=False)

    def din(name, shape, dt=F32):
        return nc.dram_tensor(name, shape, dt, kind="ExternalInput").ap()

    x_own = din("x_own", [T, D])
    x_halo = din("x_halo", [3, D])
    wip = din("wip", [D, 2 * DI])
    convw = din("convw", [DI, 4])
    convb = din("convb", [DI])
    xpw = din("xpw", [DI, 96])
    dtw = din("dtw", [RNK, DI])
    dtb = din("dtb", [DI])
    dtln = din("dtln", [RNK])
    bln = din("bln", [NST])
    cln = din("cln", [NST])
    alog = din("alog", [DI, NST])
    dskip = din("dskip", [DI])
    opw = din("opw", [DI, D])
    ilnw = din("ilnw", [D])
    plnw = din("plnw", [D])
    gw = din("gw", [D, FF])
    uw = din("uw", [D, FF])
    dwn = din("dwn", [FF, D])
    qkvw = din("qkvw", [D, 768])
    qkvb = din("qkvb", [12, 64])
    wgate = din("wgate", [D, 8])
    bgate = din("bgate", [1, 8])
    wexp = din("wexp", [8, D, D])
    bexp = din("bexp", [8, D])
    idf = din("idf", [128, 128])
    selone = din("selone", [NST, NST * 128])
    cwb = din("cwb", [128, 16])
    cmb = din("cmb", [128, 4])
    plc = din("plc", [256, D])

    out_t = nc.dram_tensor("out", [T, D], F32, kind="ExternalOutput").ap()

    def dint(name, shape, dt=F32):
        return nc.dram_tensor(name, shape, dt, kind="Internal").ap()

    dxnf = dint("dxnf", [D, T])
    dxnb = dint("dxnb", [D, T], BF16)
    dxng = dint("dxng", [4 * D, T], BF16)
    dc2i = dint("dc2i", [DI, 17])
    dc2o = dint("dc2o", [4 * DI, 17])
    dati = dint("dati", [SFULL, D], BF16)
    dato = dint("dato", [T, D], BF16)
    dcum = dint("dcum", [DI, T])
    dyp = dint("dyp", [DI, T])
    dzs = dint("dzs", [DI, T], BF16)
    dhid = dint("dhid", [D, T])
    dmoe = dint("dmoe", [T, D])
    dexp = dint("dexp", [8, D, D], BF16)
    dgw2 = dint("dgw2", [128, NC_D * FF], BF16)
    duw2 = dint("duw2", [128, NC_D * FF], BF16)
    ddw2 = dint("ddw2", [128, (FF // 128) * D], BF16)
    dsm = dint("dsm", [T, D])
    dq = dint("dq", [64, NH * SFULL], BF16)

    q = {e: [] for e in ENGS}
    sem_names = (["s_pe", "s_dve", "s_act", "s_pool", "cc1", "cc2", "cc3"]
                 + [f"q_{e}_{i}" for e in ("sync", "scalar", "gpsimd")
                    for i in range(8)])
    dma_rr = {"sync": 0, "scalar": 0, "gpsimd": 0}
    cnt = {s: 0 for s in sem_names}
    esem = {"tensor": "s_pe", "vector": "s_dve", "scalar": "s_act",
            "gpsimd": "s_pool", "sync": "s_ld"}
    writer = {}
    readers = {}
    snaps = {}

    def snap(name):
        snaps[name] = {e: len(q[e]) for e in ENGS}
    lastwait = {e: {} for e in ENGS}
    SEM = {}

    def _wait(eng, sem, val):
        if val <= 0 or lastwait[eng].get(sem, 0) >= val:
            return
        lastwait[eng][sem] = val
        q[eng].append(lambda E, sem=sem, val=val: E[eng].wait_ge(SEM[sem], val))

    def emit(eng, fn, reads=(), writes=(), dma=False, cc=None):
        deps = []
        for k in reads:
            w = writer.get(k)
            if w:
                deps.append(w)
        for k in writes:
            w = writer.get(k)
            if w:
                deps.append(w)
            deps.extend(readers.get(k, ()))
        for (peng, sem, val, isasync) in deps:
            _wait(eng, sem, val)
        if cc is not None:
            sem = cc
            cnt[sem] = 1
            inc = None
            isasync = True
        elif dma:
            slot = dma_rr[eng]
            dma_rr[eng] = (slot + 1) % 8
            sem = f"q_{eng}_{slot}"
            if cnt[sem] > 0:
                _wait(eng, sem, cnt[sem])
            cnt[sem] += 16
            inc = 16
            isasync = True
        else:
            sem = esem[eng]
            cnt[sem] += 1
            inc = 1
            isasync = False
        mark = (eng, sem, cnt[sem], isasync)
        q[eng].append(lambda E, fn=fn, sem=sem, inc=inc:
                      (fn(E[eng]).then_inc(SEM[sem], inc) if inc is not None
                       else fn(E[eng]).then_inc(SEM[sem])))
        for k in reads:
            readers.setdefault(k, []).append(mark)
        for k in writes:
            writer[k] = mark
            readers[k] = []
        return mark

    def barrier():
        vals = dict(cnt)
        for e in ENGS:
            for s, v in vals.items():
                _wait(e, s, v)

    def mm(pkey, psl, lhsT, rhs, rkeys, start, stop):
        emit("tensor", lambda E: E.matmul(psl, lhsT, rhs, start=start,
                                          stop=stop, skip_group_check=True),
             reads=rkeys, writes=(pkey,))

    def transp(pkey, psl, insl, ident, rkeys):
        emit("tensor", lambda E: E.transpose(psl, insl, ident), reads=rkeys,
             writes=(pkey,))

    def load(dst_sl, src_ap, key, reads=(), eng="sync", slow=False):
        if slow:
            emit(eng, lambda E: E.dma_start(out=dst_sl, in_=src_ap,
                                            allow_slow_non_contiguous=True),
                 reads=reads, writes=(key,), dma=True)
        else:
            emit(eng, lambda E: E.dma_start(out=dst_sl, in_=src_ap),
                 reads=reads, writes=(key,), dma=True)

    def store(dst_ap, src_sl, rkeys, wkey, eng="scalar", slow=False):
        if slow:
            emit(eng, lambda E: E.dma_start(out=dst_ap, in_=src_sl,
                                            allow_slow_non_contiguous=True),
                 reads=rkeys, writes=(wkey,), dma=True)
        else:
            emit(eng, lambda E: E.dma_start(out=dst_ap, in_=src_sl),
                 reads=rkeys, writes=(wkey,), dma=True)

    es = contextlib.ExitStack()
    with es:
        def sb(st_, name, shape, dt=F32):
            return st_.enter_context(nc.sbuf_tensor(name, shape, dt))

        for s in sem_names:
            SEM[s] = es.enter_context(nc.semaphore(s))

        csb = sb(es, "csb", [128, 10])
        ones = sb(es, "ones", [128, TS])
        id_f = sb(es, "id_f", [128, 128])
        id_b = sb(es, "id_b", [128, 128], BF16)
        sel1 = sb(es, "sel1", [NST, NST * 128], BF16)
        onesb = sb(es, "onesb", [128, 1], BF16)
        crms = sb(es, "crms", [NST, T], BF16)
        hcar = sb(es, "hcar", [128, NC_DI * NST])
        ccar = sb(es, "ccar", [128, NC_DI])
        aneg = sb(es, "aneg", [128, NC_DI * NST])
        dcolb = sb(es, "dcolb", [128, NC_DI])
        cbcol = sb(es, "cbcol", [128, NC_DI])
        dtbcol = sb(es, "dtbcol", [128, NC_DI])
        wcv = sb(es, "wcv", [128, NC_DI * 4])
        blncol = sb(es, "blncol", [NST, 1])
        clncol = sb(es, "clncol", [NST, 1])
        dlncol = sb(es, "dlncol", [RNK, 1])
        cwbs = sb(es, "cwbs", [128, 16])
        cmbs = sb(es, "cmbs", [128, 4])
        Hbuf = sb(es, "Hbuf", [128, NC_DI * NST])
        v16 = sb(es, "v16", [16, 128])
        vr1 = sb(es, "vr1", [1, 128])

        CONSTS = [0.0, 1.0, 0.125, -1.0, 1e-6, 1e-5, 1.0 / 16, 1.0 / 64,
                  1.0 / 1024]
        for i, v in enumerate(CONSTS):
            nc.const_aps.register(F32, float(v), csb[:, i : i + 1])

        PSW = es.enter_context(nc.psum_tensor("psw", [128, 2048], F32))
        PS = [PSW[:, i * 512:(i + 1) * 512] for i in range(4)]
        PS.append(es.enter_context(nc.psum_tensor("pf4", [128, 512], F32)))
        PS.append(es.enter_context(nc.psum_tensor("pf5", [128, 512], F32)))
        PB = [es.enter_context(nc.psum_tensor(f"pb{i}", [128, 128], BF16))
              for i in range(2)]
        PK = [f"PS{i}" for i in range(6)]
        PBK = [f"PB{i}" for i in range(2)]

        for i, v in enumerate(CONSTS):
            emit("vector", lambda E, i=i, v=v: E.memset(csb[:, i:i + 1],
                                                        float(v)),
                 writes=(f"c{i}",))
        emit("vector", lambda E: E.memset(ones[:, :], 1.0), writes=("ones",))
        emit("vector", lambda E: E.memset(onesb[:, :], 1.0), writes=("onesb",))
        emit("vector", lambda E: E.memset(hcar[:, :], 0.0),
             writes=tuple(f"hcar{c}_{n}" for c in range(NC_DI)
                          for n in range(NST)))
        emit("vector", lambda E: E.memset(ccar[:, :], 0.0),
             writes=tuple(f"ccar{c}" for c in range(NC_DI)))
        _wait("scalar", "s_dve", cnt["s_dve"])
        _wait("gpsimd", "s_dve", cnt["s_dve"])

        load(id_f[:, :], idf, "idf")
        emit("vector", lambda E: E.tensor_copy(id_b[:, :], id_f[:, :]),
             reads=("idf",), writes=("idb",))
        load(cwbs[:, :], cwb, "cwbs")
        load(cmbs[:, :], cmb, "cmbs")

        def vec_cols(dram_ap, dstbuf, key, nchunk):
            load(v16[0:nchunk, :], dram_ap.rearrange("(c p) -> c p", p=128),
                 "v16")
            transp(PK[5], PS[5][0:128, 0:nchunk], v16[0:nchunk, :],
                   id_f[0:nchunk, 0:nchunk], ("v16", "idf"))
            emit("vector", lambda E: E.tensor_copy(dstbuf[:, 0:nchunk],
                                                   PS[5][:, 0:nchunk]),
                 reads=(PK[5],), writes=(key, "v16_done"))

        def row_col(dram_ap, n, dstbuf, key):
            load(vr1[0:1, 0:n], dram_ap.rearrange("(o n) -> o n", o=1), "vr1")
            transp(PK[5], PS[5][0:n, 0:1], vr1[0:1, 0:n], id_f[0:1, 0:1],
                   ("vr1", "idf"))
            emit("vector", lambda E: E.tensor_copy(dstbuf[0:n, 0:1],
                                                   PS[5][0:n, 0:1]),
                 reads=(PK[5],), writes=(key, "vr1_done"))

        vec_cols(convb, cbcol, "cbcol", NC_DI)
        vec_cols(dtb, dtbcol, "dtbcol", NC_DI)
        vec_cols(dskip, dcolb, "dcolb", NC_DI)
        row_col(bln, NST, blncol, "blncol")
        row_col(cln, NST, clncol, "clncol")
        row_col(dtln, RNK, dlncol, "dlncol")

        load(wcv[:, :].rearrange("p (c k) -> p c k", c=NC_DI),
             convw.rearrange("(c p) k -> p c k", p=128), "wcv")
        load(aneg[:, :].rearrange("p (c n) -> p c n", c=NC_DI),
             alog.rearrange("(c p) n -> p c n", p=128), "aneg_raw")
        emit("scalar", lambda E: E.activation(aneg[:, :], aneg[:, :], AF.Exp),
             reads=("aneg_raw",), writes=("aneg_e",))
        emit("vector", lambda E: E.tensor_scalar(aneg[:, :], aneg[:, :], -1.0,
                                                 None, ALU.mult),
             reads=("aneg_e",), writes=("aneg",))

        a1 = contextlib.ExitStack()
        xtok = [sb(a1, "xtok0", [128, D])] * 2
        sqt = sb(a1, "sqt", [128, D])
        xnt = sb(a1, "xnt", [128, D])
        xrt = sb(a1, "xrt", [128, D], BF16)
        stat = sb(a1, "stat", [128, 8])
        xnfs = sb(a1, "xnfs", [128, NC_D * TS])
        xnbs = sb(a1, "xnbs", [128, NC_D * TS], BF16)
        xrfm = sb(a1, "xrfm", [128, NC_D * TS], BF16)
        hfm = sb(a1, "hfm", [128, NC_D * 4], BF16)
        wipb = sb(a1, "wipb", [128, NC_D * 2 * DI], BF16)
        wstg = [sb(a1, f"wstg{i}", [128, 1024]) for i in range(2)]
        ilncol = sb(a1, "ilncol", [128, NC_D])
        xpwb = sb(a1, "xpwb", [128, NC_DI * 96], BF16)
        dtwb = sb(a1, "dtwb", [RNK, DI], BF16)
        xc = sb(a1, "xc", [128, NC_DI * (TS + 3)], BF16)
        x1 = sb(a1, "x1", [128, NC_DI * TS], BF16)
        zst = [sb(a1, f"zst{i}", [128, TS], BF16) for i in range(2)]
        dts = sb(a1, "dts", [128, NC_DI * TS])
        dtxb = sb(a1, "dtxb", [128, TS])
        cumt = sb(a1, "cumt", [128, TS])
        Bb = sb(a1, "Bb", [128, NST * TS], BF16)
        Cb = sb(a1, "Cb", [128, NST * TS], BF16)
        at_ = [sb(a1, f"at{i}", [128, TS]) for i in range(2)]
        bt_ = [sb(a1, f"bt{i}", [128, TS]) for i in range(2)]
        ht16 = sb(a1, "ht16", [128, NST * TS])
        yc_ = sb(a1, "yc", [128, TS])
        hct = sb(a1, "hct", [128, TS])
        ypt = sb(a1, "ypt", [128, TS])
        u_ = sb(a1, "u_", [128, TS])
        u2_ = sb(a1, "u2_", [128, TS])
        u3_ = sb(a1, "u3_", [128, TS])
        dtr = sb(a1, "dtr", [RNK, TS])
        dtn = sb(a1, "dtn", [RNK, TS], BF16)
        br_ = sb(a1, "br_", [NST, TS])
        cr_ = sb(a1, "cr_", [NST, TS])
        brb = sb(a1, "brb", [NST, TS], BF16)
        rr_ = sb(a1, "rr_", [1, TS])
        cacc = sb(a1, "cacc", [128, TS])

        vec_cols(ilnw, ilncol, "ilncol", NC_D)
        for sh in range(2):
            load(wstg[sh][0:16, 0:1024], selone[:, sh * 1024:(sh + 1) * 1024],
                 f"wst{sh}")
            emit("vector", lambda E, sh=sh: E.tensor_copy(
                sel1[:, sh * 1024:(sh + 1) * 1024], wstg[sh][0:16, 0:1024]),
                reads=(f"wst{sh}",), writes=("sel1",))

        for c in range(NC_D):
            for hq in range(4):
                w_i = hq % 2
                load(wstg[w_i][:, :1024],
                     wip[c * 128:(c + 1) * 128,
                         hq * 1024:(hq + 1) * 1024], f"wst{w_i}")
                emit("vector", lambda E, c=c, hq=hq, w_i=w_i:
                     E.tensor_scalar(
                         wipb[:, c * 4096 + hq * 1024:
                              c * 4096 + (hq + 1) * 1024],
                         wstg[w_i][:, :1024], ilncol[:, c:c + 1], None,
                         ALU.mult),
                     reads=(f"wst{w_i}", "ilncol"), writes=(f"wipb{c}",))
        for c in range(NC_DI):
            load(wstg[c % 2][:, 0:96], xpw[c * 128:(c + 1) * 128, :],
                 f"wst{c % 2}")
            emit("vector", lambda E, c=c: E.tensor_copy(
                xpwb[:, c * 96:(c + 1) * 96], wstg[c % 2][:, 0:96]),
                reads=(f"wst{c % 2}",), writes=(f"xpwb{c}",))
        for sh in range(2):
            load(wstg[sh][0:64, 0:1024], dtw[:, sh * 1024:(sh + 1) * 1024],
                 f"wst{sh}")
            emit("vector", lambda E, sh=sh: E.tensor_scalar(
                dtwb[:, sh * 1024:(sh + 1) * 1024], wstg[sh][0:64, 0:1024],
                dlncol[:, 0:1], None, ALU.mult),
                reads=(f"wst{sh}", "dlncol"), writes=("dtwb",))

        snap("w")
        load(xtok[0][0:3, :], x_halo, "xt0")

        def ln_rows(nrow, src, srckeys):
            emit("scalar", lambda E: E.activation(sqt[0:nrow, :], src,
                                                  AF.Square),
                 reads=srckeys, writes=("sqt",))
            emit("vector", lambda E: E.reduce_sum(stat[0:nrow, 0:1], src,
                                                  axis=AX.X),
                 reads=srckeys, writes=("st0",))
            emit("vector", lambda E: E.reduce_sum(stat[0:nrow, 1:2],
                                                  sqt[0:nrow, :], axis=AX.X),
                 reads=("sqt",), writes=("st1",))
            emit("vector", lambda E: E.tensor_scalar(stat[0:nrow, 2:3],
                 stat[0:nrow, 0:1], 1.0 / D, None, ALU.mult),
                 reads=("st0",), writes=("stm",))
            emit("vector", lambda E: E.tensor_tensor(stat[0:nrow, 3:4],
                 stat[0:nrow, 2:3], stat[0:nrow, 2:3], ALU.mult),
                 reads=("stm",), writes=("stm2",))
            emit("vector", lambda E: E.tensor_scalar(stat[0:nrow, 4:5],
                 stat[0:nrow, 1:2], 1.0 / D, None, ALU.mult),
                 reads=("st1",), writes=("stv0",))
            emit("vector", lambda E: E.tensor_tensor(stat[0:nrow, 4:5],
                 stat[0:nrow, 4:5], stat[0:nrow, 3:4], ALU.subtract),
                 reads=("stv0", "stm2"), writes=("stv",))
            emit("scalar", lambda E: E.activation(stat[0:nrow, 5:6],
                 stat[0:nrow, 4:5], AF.Sqrt, bias=1e-5),
                 reads=("stv",), writes=("stsd",))
            emit("vector", lambda E: E.reciprocal(stat[0:nrow, 5:6],
                 stat[0:nrow, 5:6]),
                 reads=("stsd",), writes=("strd",))
            emit("vector", lambda E: E.tensor_scalar(xnt[0:nrow, :], src,
                 stat[0:nrow, 2:3], stat[0:nrow, 5:6], ALU.subtract, ALU.mult),
                 reads=srckeys + ("stm", "strd"), writes=("xnt",))
            emit("vector", lambda E: E.tensor_tensor(stat[0:nrow, 6:7],
                 stat[0:nrow, 5:6], stat[0:nrow, 5:6], ALU.mult),
                 reads=("strd",), writes=("sr2",))
            emit("vector", lambda E: E.tensor_tensor(stat[0:nrow, 6:7],
                 stat[0:nrow, 6:7], stat[0:nrow, 4:5], ALU.mult),
                 reads=("sr2", "stv"), writes=("svr",))
            emit("scalar", lambda E: E.activation(stat[0:nrow, 7:8],
                 stat[0:nrow, 6:7], AF.Sqrt, bias=1e-6),
                 reads=("svr",), writes=("ssd2",))
            emit("vector", lambda E: E.reciprocal(stat[0:nrow, 7:8],
                 stat[0:nrow, 7:8]),
                 reads=("ssd2",), writes=("srd2",))
            emit("vector", lambda E: E.tensor_scalar(xrt[0:nrow, :],
                 xnt[0:nrow, :], stat[0:nrow, 7:8], None, ALU.mult),
                 reads=("xnt", "srd2"), writes=("xrt",))

        ln_rows(3, xtok[0][0:3, :], ("xt0",))
        for c in range(NC_D):
            transp(PBK[0], PB[0][0:128, 0:3], xrt[0:3, c * 128:(c + 1) * 128],
                   id_b[0:3, 0:3], ("xrt", "idb"))
            emit("vector", lambda E, c=c: E.tensor_copy(
                hfm[:, c * 4:c * 4 + 3], PB[0][:, 0:3]),
                reads=(PBK[0],), writes=(f"hfm{c}",))
        for m in range(NC_DI):
            for c in range(NC_D):
                mm(PK[0], PS[0][0:128, 0:3],
                   wipb[:, c * 4096 + m * 128: c * 4096 + (m + 1) * 128],
                   hfm[:, c * 4:c * 4 + 3],
                   (f"wipb{c}", f"hfm{c}"), c == 0, c == NC_D - 1)
            emit("vector", lambda E, m=m: E.tensor_copy(
                xc[:, m * (TS + 3): m * (TS + 3) + 3], PS[0][:, 0:3]),
                reads=(PK[0],), writes=(f"xcc{m}",))

        snap("halo")
        for s in range(NSUB):
            for k in range(2):
                tt = s * 2 + k
                load(xtok[0][:, :], x_own[tt * 128:(tt + 1) * 128, :],
                     "xt0")
                ln_rows(128, xtok[0][:, :], ("xt0",))
                for c in range(NC_D):
                    transp(PK[4], PS[4][0:128, 0:128],
                           xnt[:, c * 128:(c + 1) * 128], id_f[:, :],
                           ("xnt", "idf"))
                    if s == 0 and k == 0 and c == 0:
                        snap("t1")
                    emit("vector", lambda E, c=c, k=k: E.tensor_copy(
                        xnfs[:, c * TS + k * 128: c * TS + (k + 1) * 128],
                        PS[4][:, 0:128]), reads=(PK[4],), writes=(f"xnfs{c}",))
                    if s == 0 and k == 0 and c == 0:
                        snap("t2")
                    emit("vector", lambda E, c=c, k=k: E.tensor_copy(
                        xnbs[:, c * TS + k * 128: c * TS + (k + 1) * 128],
                        PS[4][:, 0:128]), reads=(PK[4],), writes=(f"xnbs{c}",))
                    if s == 0 and k == 0 and c == 0:
                        snap("t3")
                    transp(PBK[k], PB[k][0:128, 0:128],
                           xrt[:, c * 128:(c + 1) * 128], id_b[:, :],
                           ("xrt", "idb"))
                    if s == 0 and k == 0 and c == 0:
                        snap("t4")
                    emit("vector", lambda E, c=c, k=k: E.tensor_copy(
                        xrfm[:, c * TS + k * 128: c * TS + (k + 1) * 128],
                        PB[k][:, 0:128]), reads=(PBK[k],), writes=(f"xrfm{c}",))
                    if s == 0 and k == 0 and c == 0:
                        snap("t5")
                if s == 0 and k == 0:
                    snap("t6")
            snap(f"ln{s}")
            store(dxnf[:, s * TS:(s + 1) * TS].rearrange("(c p) f -> p c f",
                                                         p=128),
                  xnfs[:, :].rearrange("p (c f) -> p c f", c=NC_D),
                  tuple(f"xnfs{c}" for c in range(NC_D)), "dxnf_st")
            store(dxnb[:, s * TS:(s + 1) * TS].rearrange("(c p) f -> p c f",
                                                         p=128),
                  xnbs[:, :].rearrange("p (c f) -> p c f", c=NC_D),
                  tuple(f"xnbs{c}" for c in range(NC_D)), "dxnb_st",
                  eng="scalar")
            for m in range(2 * NC_DI):
                pi = m % 2
                for c in range(NC_D):
                    mm(PK[pi], PS[pi][0:128, 0:TS],
                       wipb[:, c * 4096 + m * 128: c * 4096 + (m + 1) * 128],
                       xrfm[:, c * TS:(c + 1) * TS],
                       (f"wipb{c}", f"xrfm{c}"), c == 0, c == NC_D - 1)
                if m < NC_DI:
                    emit("vector", lambda E, m=m, pi=pi: E.tensor_copy(
                        xc[:, m * (TS + 3) + 3: (m + 1) * (TS + 3)],
                        PS[pi][:, 0:TS]), reads=(PK[pi],), writes=(f"xcc{m}",))
                else:
                    zc = m - NC_DI
                    emit("scalar", lambda E, zc=zc, pi=pi: E.activation(
                        zst[zc % 2][:, :], PS[pi][:, 0:TS], AF.Silu),
                        reads=(PK[pi],), writes=(f"zst{zc % 2}",))
                    store(dzs[zc * 128:(zc + 1) * 128, s * TS:(s + 1) * TS],
                          zst[zc % 2][:, :], (f"zst{zc % 2}",), "dzs_st",
                          eng="scalar")
            snap(f"ip{s}")
            for c in range(NC_DI):
                base = c * (TS + 3)
                emit("vector", lambda E, c=c, base=base: E.tensor_scalar(
                    cacc[:, :], xc[:, base:base + TS],
                    wcv[:, c * 4:c * 4 + 1], None, ALU.mult),
                    reads=(f"xcc{c}", "wcv"), writes=("cacc",))
                for kk in range(1, 4):
                    emit("vector", lambda E, c=c, base=base, kk=kk:
                         E.scalar_tensor_tensor(cacc[:, :],
                             xc[:, base + kk:base + kk + TS],
                             wcv[:, c * 4 + kk:c * 4 + kk + 1],
                             cacc[:, :], ALU.mult, ALU.add),
                         reads=(f"xcc{c}", "wcv", "cacc"), writes=("cacc",))
                emit("scalar", lambda E, c=c: E.activation(
                    x1[:, c * TS:(c + 1) * TS], cacc[:, :], AF.Silu,
                    bias=cbcol[:, c:c + 1]),
                    reads=("cacc", "cbcol"), writes=(f"x1{c}",))
                emit("gpsimd", lambda E, base=base: E.tensor_copy(
                    xc[:, base:base + 3], xc[:, base + TS:base + TS + 3]),
                    reads=(f"xcc{c}",), writes=(f"xcc{c}",))
            snap(f"cv{s}")
            for c in range(NC_DI):
                st0, sp0 = (c == 0), (c == NC_DI - 1)
                mm(PK[2], PS[2][0:RNK, 0:TS], xpwb[:, c * 96:c * 96 + 64],
                   x1[:, c * TS:(c + 1) * TS], (f"xpwb{c}", f"x1{c}"), st0,
                   sp0)
                mm(PK[3], PS[3][0:NST, 0:TS], xpwb[:, c * 96 + 64:c * 96 + 80],
                   x1[:, c * TS:(c + 1) * TS], (f"xpwb{c}", f"x1{c}"), st0,
                   sp0)
                mm(PK[4], PS[4][0:NST, 0:TS], xpwb[:, c * 96 + 80:c * 96 + 96],
                   x1[:, c * TS:(c + 1) * TS], (f"xpwb{c}", f"x1{c}"), st0,
                   sp0)

            def rms_small(pidx, nrow, rawbuf, outsl, okey, lncol, lnkey,
                          scale):
                emit("vector", lambda E: E.tensor_copy(rawbuf[0:nrow, :],
                                                       PS[pidx][0:nrow, 0:TS]),
                     reads=(PK[pidx],), writes=(okey + "_raw",))
                emit("scalar", lambda E: E.activation(u3_[0:nrow, :],
                     rawbuf[0:nrow, :], AF.Square),
                     reads=(okey + "_raw",), writes=(okey + "_sq",))
                mm(PK[5], PS[5][0:1, 0:TS], ones[0:nrow, 0:1],
                   u3_[0:nrow, :], (okey + "_sq", "ones"), True, True)
                emit("scalar", lambda E, scale=scale: E.activation(
                    rr_[0:1, :], PS[5][0:1, 0:TS], AF.Sqrt, bias=1e-6,
                    scale=scale),
                    reads=(PK[5],), writes=(okey + "_sd",))
                emit("vector", lambda E: E.reciprocal(rr_[0:1, :],
                                                      rr_[0:1, :]),
                     reads=(okey + "_sd",), writes=(okey + "_rs",))
                mm(PK[5], PS[5][0:nrow, 0:TS], ones[0:1, 0:nrow],
                   rr_[0:1, :], (okey + "_rs", "ones"), True, True)
                if lncol is None:
                    emit("vector", lambda E: E.tensor_tensor(
                        outsl, rawbuf[0:nrow, :], PS[5][0:nrow, 0:TS],
                        ALU.mult),
                        reads=(okey + "_raw", PK[5]), writes=(okey,))
                else:
                    emit("vector", lambda E: E.tensor_tensor(
                        rawbuf[0:nrow, :], rawbuf[0:nrow, :],
                        PS[5][0:nrow, 0:TS], ALU.mult),
                        reads=(okey + "_raw", PK[5]), writes=(okey + "_n",))
                    emit("vector", lambda E: E.tensor_scalar(
                        outsl, rawbuf[0:nrow, :], lncol, None, ALU.mult),
                        reads=(okey + "_n", lnkey), writes=(okey,))

            rms_small(2, RNK, dtr, dtn[:, :], "dtn", None, None, 1.0 / 64)
            rms_small(3, NST, br_, brb[0:NST, :], "brms", blncol[:, 0:1],
                      "blncol", 1.0 / 16)
            rms_small(4, NST, cr_, crms[:, s * TS:(s + 1) * TS], "crms",
                      clncol[:, 0:1], "clncol", 1.0 / 16)
            snap(f"xp{s}")
            for n in range(NST):
                mm(PK[2], PS[2][0:128, 0:TS], sel1[:, n * 128:(n + 1) * 128],
                   brb[0:NST, :], ("brms", "sel1"), True, True)
                emit("scalar", lambda E, n=n: E.copy(
                    Bb[:, n * TS:(n + 1) * TS], PS[2][:, 0:TS]),
                    reads=(PK[2],), writes=(f"Bb{n}",))
                mm(PK[3], PS[3][0:128, 0:TS], sel1[:, n * 128:(n + 1) * 128],
                   crms[:, s * TS:(s + 1) * TS], ("crms", "sel1"), True, True)
                emit("vector", lambda E, n=n: E.tensor_copy(
                    Cb[:, n * TS:(n + 1) * TS], PS[3][:, 0:TS]),
                    reads=(PK[3],), writes=(f"Cb{n}",))
            snap(f"bc{s}")
            for c in range(NC_DI):
                pi = 2 + (c % 2)
                mm(PK[pi], PS[pi][0:128, 0:TS], dtwb[:, c * 128:(c + 1) * 128],
                   dtn[:, :], ("dtwb", "dtn"), True, True)
                emit("vector", lambda E, pi=pi, c=c: E.tensor_scalar(
                    u_[:, :], PS[pi][:, 0:TS], dtbcol[:, c:c + 1], None,
                    ALU.add),
                    reads=(PK[pi], "dtbcol"), writes=("spu",))
                emit("scalar", lambda E: E.activation(u2_[:, :], u_[:, :],
                                                      AF.Abs),
                     reads=("spu",), writes=("spa",))
                emit("scalar", lambda E: E.activation(u2_[:, :], u2_[:, :],
                                                      AF.Exp, scale=-1.0),
                     reads=("spa",), writes=("spe",))
                emit("scalar", lambda E: E.activation(u2_[:, :], u2_[:, :],
                                                      AF.Ln, bias=1.0),
                     reads=("spe",), writes=("spl",))
                emit("vector", lambda E: E.tensor_scalar(u3_[:, :], u_[:, :],
                                                         0.0, None, ALU.max),
                     reads=("spu",), writes=("spr",))
                emit("vector", lambda E, c=c: E.tensor_tensor(
                    dts[:, c * TS:(c + 1) * TS], u3_[:, :], u2_[:, :],
                    ALU.add),
                    reads=("spr", "spl"), writes=(f"dts{c}",))
            snap(f"dt{s}")
            for c in range(NC_DI):
                emit("vector", lambda E, c=c: E.tensor_tensor(
                    dtxb[:, :], dts[:, c * TS:(c + 1) * TS],
                    x1[:, c * TS:(c + 1) * TS], ALU.mult),
                    reads=(f"dts{c}", f"x1{c}"), writes=("dtx",))
                emit("vector", lambda E, c=c: E.tensor_tensor_scan(
                    cumt[:, :], ones[:, 0:TS], dts[:, c * TS:(c + 1) * TS],
                    ccar[:, c:c + 1], ALU.mult, ALU.add),
                    reads=(f"dts{c}", "ones", f"ccar{c}"), writes=("cumt",))
                emit("vector", lambda E, c=c: E.tensor_copy(
                    ccar[:, c:c + 1], cumt[:, TS - 1:TS]),
                    reads=("cumt",), writes=(f"ccar{c}",))
                store(dcum[c * 128:(c + 1) * 128, s * TS:(s + 1) * TS],
                      cumt[:, :], ("cumt",), "dcum_st")
                for n in range(NST):
                    ab = at_[n % 2]
                    bb = bt_[n % 2]
                    hsl = ht16[:, n * TS:(n + 1) * TS]
                    beng = "vector"
                    emit("scalar", lambda E, c=c, n=n, ab=ab: E.activation(
                        ab[:, :], dts[:, c * TS:(c + 1) * TS], AF.Exp,
                        scale=aneg[:, c * NST + n:c * NST + n + 1]),
                        reads=(f"dts{c}", "aneg"), writes=(f"at{n % 2}",))
                    emit(beng, lambda E, n=n, bb=bb: E.tensor_tensor(
                        bb[:, :], dtxb[:, :], Bb[:, n * TS:(n + 1) * TS],
                        ALU.mult),
                        reads=("dtx", f"Bb{n}"), writes=(f"bt{n % 2}",))
                    emit("vector", lambda E, c=c, n=n, ab=ab, bb=bb, hsl=hsl:
                         E.tensor_tensor_scan(hsl, ab[:, :], bb[:, :],
                             hcar[:, c * NST + n:c * NST + n + 1],
                             ALU.mult, ALU.add),
                         reads=(f"at{n % 2}", f"bt{n % 2}", f"hcar{c}_{n}"),
                         writes=(f"ht{n}",))
                    if n == 0:
                        emit("gpsimd", lambda E, n=n, hsl=hsl: E.tensor_tensor(
                            yc_[:, :], hsl, Cb[:, n * TS:(n + 1) * TS],
                            ALU.mult),
                            reads=(f"ht{n}", f"Cb{n}"), writes=("yc",))
                    else:
                        emit("gpsimd", lambda E, n=n, hsl=hsl: E.tensor_tensor(
                            hct[:, :], hsl, Cb[:, n * TS:(n + 1) * TS],
                            ALU.mult),
                            reads=(f"ht{n}", f"Cb{n}"), writes=("hctk",))
                        emit("gpsimd", lambda E: E.tensor_tensor(
                            yc_[:, :], yc_[:, :], hct[:, :], ALU.add),
                            reads=("yc", "hctk"), writes=("yc",))
                emit("vector", lambda E, c=c: E.tensor_copy(
                    hcar[:, c * NST:(c + 1) * NST],
                    ht16[:, TS - 1:NST * TS:TS]),
                    reads=tuple(f"ht{n}" for n in range(NST)),
                    writes=tuple(f"hcar{c}_{n}" for n in range(NST)))
                emit("vector", lambda E, c=c: E.scalar_tensor_tensor(
                    ypt[:, :], x1[:, c * TS:(c + 1) * TS], dcolb[:, c:c + 1],
                    yc_[:, :], ALU.mult, ALU.add),
                    reads=(f"x1{c}", "dcolb", "yc"), writes=("ypt",))
                store(dyp[c * 128:(c + 1) * 128, s * TS:(s + 1) * TS],
                      ypt[:, :], ("ypt",), "dyp_st")

            snap(f"s{s}")
        HKEYS = tuple(f"hcar{c}_{n}" for c in range(NC_DI)
                      for n in range(NST))
        CKEYS = tuple(f"ccar{c}" for c in range(NC_DI))
        store(dc2i[:, 0:16].rearrange("(c p) n -> p c n", p=128),
              hcar[:, :].rearrange("p (c n) -> p c n", c=NC_DI),
              HKEYS, "dc2i_h", slow=True)
        store(dc2i[:, 16:17].rearrange("(c p) n -> p c n", p=128),
              ccar[:, :].rearrange("p (c o) -> p c o", o=1),
              CKEYS, "dc2i_c", slow=True)
        emit("gpsimd", lambda E: E.collective_compute(
            "AllGather", ALU.bypass, replica_groups=RG,
            ins=[dc2i.opt()], outs=[dc2o.opt()]),
            reads=("dc2i_h", "dc2i_c"), writes=("dc2o",), cc="cc2")
        emit("gpsimd", lambda E: E.collective_compute(
            "AllGather", ALU.bypass, replica_groups=RG,
            ins=[dxnb.opt()], outs=[dxng.opt()]),
            reads=("dxnb_st",), writes=("dxng",), cc="cc1")

        snap("cc")
        barrier()
        a1.close()

        hnst = contextlib.ExitStack()
        hn = sb(hnst, "hn", [128, NC_D * T], BF16)

        b1 = contextlib.ExitStack()
        hg = [sb(b1, f"hg{r}", [128, NC_DI * 17]) for r in range(4)]
        sih = sb(b1, "sih", [128, 2])
        et_ = sb(b1, "et_", [128, NST])
        opwb = sb(b1, "opwb", [128, NC_DI * D], BF16)
        cumb = [sb(b1, f"cumb{i}", [128, 512]) for i in range(2)]
        ypb = [sb(b1, f"ypb{i}", [128, 512]) for i in range(2)]
        yfb = sb(b1, "yfb", [128, NC_DI * 512], BF16)
        zsb = sb(b1, "zsb", [128, 512], BF16)
        cb2all = sb(b1, "cb2all", [128, NST * 512], BF16)
        ee_ = sb(b1, "ee_", [128, 512])
        tt_ = sb(b1, "tt_", [128, 512])
        xnfb = [sb(b1, f"xnfb{i}", [128, 512]) for i in range(2)]
        hidb = sb(b1, "hidb", [128, NC_D * 512])
        sqb = sb(b1, "sqb", [128, 512], BF16)
        rsr = sb(b1, "rsr", [1, 512])
        wst2 = [sb(b1, f"wst2{i}", [128, 1024]) for i in range(2)]
        plncol = sb(b1, "plncol", [128, NC_D])

        for r in range(4):
            load(hg[r][:, :].rearrange("p (c v) -> p c v", c=NC_DI),
                 dc2o[r * DI:(r + 1) * DI, :].rearrange("(c p) v -> p c v",
                                                        p=128),
                 f"hg{r}", reads=("dc2o",))
        for c in range(NC_DI):
            emit("vector", lambda E, c=c: E.memset(
                Hbuf[:, c * NST:(c + 1) * NST], 0.0), writes=(f"H{c}",))
            for r in range(4):
                emit("vector", lambda E, c=c, r=r: E.tensor_scalar(
                    sih[:, 0:1], hg[0][:, c * 17 + 16:c * 17 + 17],
                    cwbs[:, r * 4:r * 4 + 1], None, ALU.mult),
                    reads=("hg0", "cwbs"), writes=("sih",))
                for kk in range(1, 4):
                    emit("vector", lambda E, c=c, r=r, kk=kk:
                         E.scalar_tensor_tensor(sih[:, 0:1],
                             hg[kk][:, c * 17 + 16:c * 17 + 17],
                             cwbs[:, r * 4 + kk:r * 4 + kk + 1],
                             sih[:, 0:1], ALU.mult, ALU.add),
                         reads=(f"hg{kk}", "cwbs", "sih"), writes=("sih",))
                emit("scalar", lambda E, c=c: E.activation(
                    et_[:, :], aneg[:, c * NST:(c + 1) * NST], AF.Exp,
                    scale=sih[:, 0:1]),
                    reads=("aneg", "sih"), writes=("et",))
                emit("vector", lambda E, c=c, r=r: E.tensor_tensor(
                    et_[:, :], et_[:, :], hg[r][:, c * 17:c * 17 + 16],
                    ALU.mult),
                    reads=("et", f"hg{r}"), writes=("et2",))
                emit("vector", lambda E, c=c, r=r: E.scalar_tensor_tensor(
                    Hbuf[:, c * NST:(c + 1) * NST], et_[:, :],
                    cmbs[:, r:r + 1], Hbuf[:, c * NST:(c + 1) * NST],
                    ALU.mult, ALU.add),
                    reads=("et2", "cmbs", f"H{c}"), writes=(f"H{c}",))

        snap("carry")
        vec_cols(plnw, plncol, "plncol", NC_D)
        for c in range(NC_DI):
            load(wst2[c % 2][:, 0:1024], opw[c * 128:(c + 1) * 128, :],
                 f"w2{c % 2}")
            emit("vector", lambda E, c=c: E.tensor_copy(
                opwb[:, c * 1024:(c + 1) * 1024], wst2[c % 2][:, 0:1024]),
                reads=(f"w2{c % 2}",), writes=(f"opwb{c}",))

        for tb in range(4):
            for n in range(NST):
                mm(PK[3], PS[3][0:128, 0:512], sel1[:, n * 128:(n + 1) * 128],
                   crms[:, tb * 512:(tb + 1) * 512], ("crms", "sel1"), True,
                   True)
                emit("vector", lambda E, n=n: E.tensor_copy(
                    cb2all[:, n * 512:(n + 1) * 512], PS[3][:, 0:512]),
                    reads=(PK[3],), writes=(f"cb2_{n}",))
            for c in range(NC_DI):
                ci = c % 2
                load(cumb[ci][:, :],
                     dcum[c * 128:(c + 1) * 128, tb * 512:(tb + 1) * 512],
                     f"cumb{ci}", reads=("dcum_st",))
                load(ypb[ci][:, :],
                     dyp[c * 128:(c + 1) * 128, tb * 512:(tb + 1) * 512],
                     f"ypb{ci}", reads=("dyp_st",))
                for n in range(NST):
                    emit("scalar", lambda E, c=c, n=n, ci=ci: E.activation(
                        ee_[:, :], cumb[ci][:, :], AF.Exp,
                        scale=aneg[:, c * NST + n:c * NST + n + 1]),
                        reads=(f"cumb{ci}", "aneg"), writes=("ee",))
                    emit("vector", lambda E, c=c, n=n: E.scalar_tensor_tensor(
                        tt_[:, :], ee_[:, :],
                        Hbuf[:, c * NST + n:c * NST + n + 1],
                        cb2all[:, n * 512:(n + 1) * 512],
                        ALU.mult, ALU.mult),
                        reads=("ee", f"H{c}", f"cb2_{n}"), writes=("ttc",))
                    emit("gpsimd", lambda E, ci=ci: E.tensor_tensor(
                        ypb[ci][:, :], ypb[ci][:, :], tt_[:, :], ALU.add),
                        reads=("ttc", f"ypb{ci}"), writes=(f"ypb{ci}",))
                load(zsb[:, :], dzs[c * 128:(c + 1) * 128,
                                    tb * 512:(tb + 1) * 512], "zsb",
                     reads=("dzs_st",))
                emit("vector", lambda E, c=c, ci=ci: E.tensor_tensor(
                    yfb[:, c * 512:(c + 1) * 512], ypb[ci][:, :], zsb[:, :],
                    ALU.mult),
                    reads=(f"ypb{ci}", "zsb"), writes=(f"yfb{c}",))
            for mo in range(NC_D):
                pi = mo % 2
                for c in range(NC_DI):
                    mm(PK[pi], PS[pi][0:128, 0:512],
                       opwb[:, c * 1024 + mo * 128: c * 1024 + (mo + 1) * 128],
                       yfb[:, c * 512:(c + 1) * 512],
                       (f"opwb{c}", f"yfb{c}"), c == 0, c == NC_DI - 1)
                load(xnfb[pi][:, :], dxnf[mo * 128:(mo + 1) * 128,
                                          tb * 512:(tb + 1) * 512],
                     f"xnfb{pi}", reads=("dxnf_st",))
                emit("vector", lambda E, mo=mo, pi=pi: E.tensor_tensor(
                    hidb[:, mo * 512:(mo + 1) * 512], PS[pi][:, 0:512],
                    xnfb[pi][:, :], ALU.add),
                    reads=(PK[pi], f"xnfb{pi}"), writes=(f"hidb{mo}",))
                store(dhid[mo * 128:(mo + 1) * 128, tb * 512:(tb + 1) * 512],
                      hidb[:, mo * 512:(mo + 1) * 512], (f"hidb{mo}",),
                      "dhid_st")
                emit("scalar", lambda E, mo=mo: E.activation(
                    sqb[:, :], hidb[:, mo * 512:(mo + 1) * 512], AF.Square),
                    reads=(f"hidb{mo}",), writes=("sqb",))
                mm(PK[5], PS[5][0:1, 0:512], onesb[:, 0:1], sqb[:, :],
                   ("sqb", "onesb"), mo == 0, mo == NC_D - 1)
            emit("scalar", lambda E: E.activation(rsr[0:1, :],
                 PS[5][0:1, 0:512], AF.Sqrt, bias=1e-6, scale=1.0 / 1024),
                 reads=(PK[5],), writes=("rsd",))
            emit("vector", lambda E: E.reciprocal(rsr[0:1, :], rsr[0:1, :]),
                 reads=("rsd",), writes=("rsrk",))
            mm(PK[4], PS[4][0:128, 0:512], ones[0:1, 0:128], rsr[0:1, :],
               ("rsrk", "ones"), True, True)
            for mo in range(NC_D):
                emit("vector", lambda E, mo=mo, tb=tb: E.tensor_tensor(
                    hn[:, mo * T + tb * 512: mo * T + (tb + 1) * 512],
                    hidb[:, mo * 512:(mo + 1) * 512], PS[4][:, 0:512],
                    ALU.mult),
                    reads=(f"hidb{mo}", PK[4], "dhid_st"),
                    writes=(f"hn{mo}_{tb}",))

        snap("b1")
        barrier()
        b1.close()

        a2 = contextlib.ExitStack()
        wst2a = [sb(a2, f"wst2a{i}", [128, 1024]) for i in range(2)]
        plncol2 = sb(a2, "plncol2", [128, NC_D])
        wge = [sb(a2, f"wge{i}", [128, 4096], BF16) for i in range(2)]
        xts = a2.enter_context(nc.sbuf_tensor("xts", [128, NC_D * 128], BF16))
        wgs = a2.enter_context(nc.sbuf_tensor("wgs", [128, NC_D * 8], F32))
        xtf = a2.enter_context(nc.sbuf_tensor("xtf", [128, NC_D * 128], F32))
        bgb = sb(a2, "bgb", [128, 8])
        bgr = sb(a2, "bgr", [1, 8])
        glog = sb(a2, "glog", [128, 8])
        gstat = sb(a2, "gstat", [128, 6])
        gsel = sb(a2, "gsel", [128, 8])
        gtmp = sb(a2, "gtmp", [128, 8])
        comb = sb(a2, "comb", [128, 16 * 8])
        combT = sb(a2, "combT", [8, 16 * 128], BF16)
        bexps = sb(a2, "bexps", [8, D], BF16)
        webb = sb(a2, "webb", [128, NC_D * 1024], BF16)
        macc = sb(a2, "macc", [128, 4 * 1024])
        mev = sb(a2, "mev", [128, 512])

        vec_cols(plnw, plncol2, "plncol2", NC_D)

        for c in range(NC_D):
            for src_w, dst_w, skey in ((gw, dgw2, "dgw2_st"),
                                       (uw, duw2, "duw2_st")):
                wi = c % 2
                for hq in range(4):
                    load(wst2a[hq % 2][:, :], src_w[c * 128:(c + 1) * 128,
                         hq * 1024:(hq + 1) * 1024], f"w2a{hq % 2}")
                    emit("vector", lambda E, c=c, wi=wi, hq=hq:
                         E.tensor_scalar(wge[wi][:, hq * 1024:
                                               (hq + 1) * 1024],
                             wst2a[hq % 2][:, :], plncol2[:, c:c + 1], None,
                             ALU.mult),
                         reads=(f"w2a{hq % 2}", "plncol2"), writes=(f"wge{wi}",))
                store(dst_w[:, c * 4096:(c + 1) * 4096], wge[wi][:, :],
                      (f"wge{wi}",), skey)
        for c in range(FF // 128):
            load(wst2a[c % 2][:, 0:1024], dwn[c * 128:(c + 1) * 128, :],
                 f"w2a{c % 2}")
            emit("gpsimd", lambda E, c=c: E.tensor_copy(
                wge[c % 2][:, 0:1024], wst2a[c % 2][:, 0:1024]),
                reads=(f"w2a{c % 2}",), writes=(f"wge{c % 2}",))
            store(ddw2[:, c * 1024:(c + 1) * 1024], wge[c % 2][:, 0:1024],
                  (f"wge{c % 2}",), "ddw2_st")
        for e in range(8):
            for c in range(NC_D):
                load(wst2a[c % 2][:, 0:1024], wexp[e, c * 128:(c + 1) * 128, :],
                     f"w2a{c % 2}")
                emit("gpsimd", lambda E, c=c: E.tensor_copy(
                    wge[c % 2][:, 0:1024], wst2a[c % 2][:, 0:1024]),
                    reads=(f"w2a{c % 2}",), writes=(f"wge{c % 2}",))
                store(dexp[e, c * 128:(c + 1) * 128, :], wge[c % 2][:, 0:1024],
                      (f"wge{c % 2}",), "dexp_st")

        load(wst2a[0][:, 0:8 * NC_D].rearrange("p (c e) -> p c e", c=NC_D),
             wgate.rearrange("(c p) e -> p c e", p=128), "w2a0")
        emit("vector", lambda E: E.tensor_copy(wgs[:, :],
                                               wst2a[0][:, 0:8 * NC_D]),
             reads=("w2a0",), writes=("wgs",))
        load(bgr[0:1, :], bgate, "bgr")
        mm(PK[5], PS[5][0:128, 0:8], ones[0:1, 0:128], bgr[0:1, :],
           ("bgr", "ones"), True, True)
        emit("vector", lambda E: E.tensor_copy(bgb[:, :], PS[5][:, 0:8]),
             reads=(PK[5],), writes=("bgb",))
        load(wst2a[0][0:8, :], bexp.rearrange("(o e) f -> o e f", o=1)[0],
             "w2a0")
        emit("vector", lambda E: E.tensor_copy(bexps[:, :], wst2a[0][0:8, :]),
             reads=("w2a0",), writes=("bexps",))

        for tt in range(16):
            for c in range(NC_D):
                load(xtf[:, c * 128:(c + 1) * 128],
                     dxnf[c * 128:(c + 1) * 128, tt * 128:(tt + 1) * 128],
                     f"xtf{c}", reads=("dxnf_st",))
                mm(PK[2], PS[2][0:128, 0:8], xtf[:, c * 128:(c + 1) * 128],
                   wgs[:, c * 8:(c + 1) * 8], (f"xtf{c}", "wgs"), c == 0,
                   c == NC_D - 1)
            emit("vector", lambda E: E.tensor_tensor(glog[:, :], PS[2][:, 0:8],
                                                     bgb[:, :], ALU.add),
                 reads=(PK[2], "bgb"), writes=("glog",))
            emit("vector", lambda E: E.reduce_max(gstat[:, 0:1], glog[:, :],
                                                  axis=AX.X),
                 reads=("glog",), writes=("gm",))
            emit("vector", lambda E: E.tensor_scalar(gstat[:, 1:2],
                 gstat[:, 0:1], -1.0, None, ALU.mult),
                 reads=("gm",), writes=("gmn",))
            emit("scalar", lambda E: E.activation(glog[:, :], glog[:, :],
                 AF.Exp, bias=gstat[:, 1:2]),
                 reads=("glog", "gmn"), writes=("glog",))
            emit("vector", lambda E: E.reduce_sum(gstat[:, 2:3], glog[:, :],
                                                  axis=AX.X),
                 reads=("glog",), writes=("gz",))
            emit("vector", lambda E: E.reciprocal(gstat[:, 2:3],
                                                  gstat[:, 2:3]),
                 reads=("gz",), writes=("gzr",))
            emit("vector", lambda E: E.tensor_scalar(glog[:, :], glog[:, :],
                 gstat[:, 2:3], None, ALU.mult),
                 reads=("glog", "gzr"), writes=("gate_p",))
            emit("vector", lambda E: E.reduce_max(gstat[:, 3:4], glog[:, :],
                                                  axis=AX.X),
                 reads=("gate_p",), writes=("gm1",))
            emit("vector", lambda E: E.tensor_single_scalar(gsel[:, :],
                 glog[:, :], gstat[:, 3:4], ALU.is_ge),
                 reads=("gate_p", "gm1"), writes=("geq",))
            emit("vector", lambda E: E.tensor_scalar(gtmp[:, :], gsel[:, :],
                 gstat[:, 3:4], None, ALU.mult),
                 reads=("geq", "gm1"), writes=("geqm",))
            emit("vector", lambda E: E.tensor_tensor(gtmp[:, :], glog[:, :],
                 gtmp[:, :], ALU.subtract),
                 reads=("gate_p", "geqm"), writes=("gmask",))
            emit("vector", lambda E: E.reduce_max(gstat[:, 5:6], gtmp[:, :],
                                                  axis=AX.X),
                 reads=("gmask",), writes=("gm2",))
            emit("vector", lambda E: E.tensor_single_scalar(gsel[:, :],
                 glog[:, :], gstat[:, 5:6], ALU.is_ge),
                 reads=("gate_p", "gm2"), writes=("gsel",))
            emit("vector", lambda E, tt=tt: E.tensor_tensor(
                comb[:, tt * 8:(tt + 1) * 8], glog[:, :], gsel[:, :],
                ALU.mult),
                reads=("gate_p", "gsel"), writes=(f"comb{tt}",))
            transp(PK[3], PS[3][0:8, 0:128], comb[:, tt * 8:(tt + 1) * 8],
                   id_f[:, :], (f"comb{tt}", "idf"))
            emit("vector", lambda E, tt=tt: E.tensor_copy(
                combT[:, tt * 128:(tt + 1) * 128], PS[3][0:8, 0:128]),
                reads=(PK[3],), writes=(f"combT{tt}",))

        for g in range(4):
            for t4 in range(4):
                tt = g * 4 + t4
                for hh in range(2):
                    mm(PK[4], PS[4][0:128, 0:512],
                       combT[:, tt * 128:(tt + 1) * 128],
                       bexps[:, hh * 512:(hh + 1) * 512],
                       (f"combT{tt}", "bexps"), True, True)
                    emit("vector", lambda E, t4=t4, hh=hh: E.tensor_copy(
                        macc[:, t4 * 1024 + hh * 512: t4 * 1024 +
                             (hh + 1) * 512], PS[4][:, 0:512]),
                        reads=(PK[4],), writes=(f"macc{t4}",))
            for e in range(8):
                for c in range(NC_D):
                    load(webb[:, c * 1024:(c + 1) * 1024],
                         dexp[e, c * 128:(c + 1) * 128, :], "webb",
                         reads=("dexp_st",))
                for t4 in range(4):
                    tt = g * 4 + t4
                    for c in range(NC_D):
                        load(xts[:, c * 128:(c + 1) * 128],
                             dxnb[c * 128:(c + 1) * 128,
                                  tt * 128:(tt + 1) * 128], f"xts{c}",
                             reads=("dxnb_st",))
                    for hh in range(2):
                        pi = hh
                        for c in range(NC_D):
                            mm(PK[pi], PS[pi][0:128, 0:512],
                               xts[:, c * 128:(c + 1) * 128],
                               webb[:, c * 1024 + hh * 512:
                                    c * 1024 + (hh + 1) * 512],
                               (f"xts{c}", "webb"), c == 0,
                               c == NC_D - 1)
                        emit("vector", lambda E, pi=pi, e=e, tt=tt:
                             E.tensor_scalar(mev[:, :], PS[pi][:, 0:512],
                                 comb[:, tt * 8 + e: tt * 8 + e + 1], None,
                                 ALU.mult),
                             reads=(PK[pi], f"comb{tt}"), writes=("mev",))
                        emit("vector", lambda E, t4=t4, hh=hh:
                             E.tensor_tensor(
                                 macc[:, t4 * 1024 + hh * 512: t4 * 1024 +
                                      (hh + 1) * 512],
                                 macc[:, t4 * 1024 + hh * 512: t4 * 1024 +
                                      (hh + 1) * 512], mev[:, :], ALU.add),
                             reads=("mev", f"macc{t4}"), writes=(f"macc{t4}",))
            for t4 in range(4):
                tt = g * 4 + t4
                store(dmoe[tt * 128:(tt + 1) * 128, :],
                      macc[:, t4 * 1024:(t4 + 1) * 1024], (f"macc{t4}",),
                      "dmoe_st")

        snap("a2")
        barrier()
        a2.close()

        cs = contextlib.ExitStack()
        qw = sb(cs, "qw", [128, NC_D * 768], BF16)
        bqT = sb(cs, "bqT", [64, 12])
        bqs = sb(cs, "bqs", [12, 64])
        qst = [sb(cs, f"qst{i}", [64, 512], BF16) for i in range(2)]
        qld = [sb(cs, f"qld{i}", [64, 512], BF16) for i in range(2)]
        Kh = sb(cs, "Kh", [64, SFULL], BF16)
        Vf = sb(cs, "Vf", [64, SFULL], BF16)
        Vt = sb(cs, "Vt", [128, 64 * 65], BF16)
        xa = sb(cs, "xa", [128, NC_D * 512], BF16)
        Pt = [sb(cs, f"Pt{i}", [128, 2048], BF16) for i in range(4)]
        sO = [sb(cs, f"sO{i}", [65, 512]) for i in range(2)]
        rz_ = sb(cs, "rz_", [128, 1])
        aac = sb(cs, "aac", [128, 64 * 256], BF16)
        aaT = sb(cs, "aaT", [128, 256], BF16)
        plcs = sb(cs, "plcs", [128, 2 * D], BF16)
        afull = sb(cs, "afull", [128, D], BF16)
        wstc = [sb(cs, f"wstc{i}", [128, 768]) for i in range(2)]

        for c in range(NC_D):
            load(wstc[c % 2][:, :], qkvw[c * 128:(c + 1) * 128, :],
                 f"wc{c % 2}")
            emit("vector", lambda E, c=c: E.tensor_copy(
                qw[:, c * 768:(c + 1) * 768], wstc[c % 2][:, :]),
                reads=(f"wc{c % 2}",), writes=(f"qw{c}",))
        load(bqs[:, :], qkvb, "bqs_r")
        transp(PK[5], PS[5][0:64, 0:12], bqs[:, :], id_f[0:12, 0:12],
               ("bqs_r", "idf"))
        emit("vector", lambda E: E.tensor_copy(bqT[:, :], PS[5][0:64, 0:12]),
             reads=(PK[5],), writes=("bqT",))
        for c2 in range(2):
            load(wstc[0][:, 0:768], plc[c2 * 128:(c2 + 1) * 128, 0:768],
                 "wc0")
            load(wstc[1][:, 0:256], plc[c2 * 128:(c2 + 1) * 128, 768:1024],
                 "wc1")
            emit("vector", lambda E, c2=c2: E.tensor_copy(
                plcs[:, c2 * D:c2 * D + 768], wstc[0][:, 0:768]),
                reads=("wc0",), writes=("plcs",))
            emit("vector", lambda E, c2=c2: E.tensor_copy(
                plcs[:, c2 * D + 768:(c2 + 1) * D], wstc[1][:, 0:256]),
                reads=("wc1",), writes=("plcs",))

        for r in range(4):
            for tt in range(4):
                for c in range(NC_D):
                    load(xa[:, c * 512:(c + 1) * 512],
                         dxng[r * D + c * 128: r * D + (c + 1) * 128,
                              tt * 512:(tt + 1) * 512], f"xa{c}",
                         reads=("dxng",))
                for h in range(NH):
                    for c in range(NC_D):
                        mm(PK[2], PS[2][0:64, 0:512],
                           qw[:, c * 768 + h * 64: c * 768 + (h + 1) * 64],
                           xa[:, c * 512:(c + 1) * 512],
                           (f"qw{c}", f"xa{c}"), c == 0, c == NC_D - 1)
                    qi = (r * 4 + tt) % 2
                    emit("vector", lambda E, h=h, qi=qi: E.tensor_scalar(
                        qst[qi][:, :], PS[2][0:64, 0:512], bqT[:, h:h + 1],
                        None, ALU.add),
                        reads=(PK[2], "bqT"), writes=(f"qst{qi}",))
                    store(dq[:, h * SFULL + r * 2048 + tt * 512:
                             h * SFULL + r * 2048 + (tt + 1) * 512],
                          qst[qi][:, :], (f"qst{qi}",), "dq_st", eng="scalar")

        snap("qs")
        for h in range(NH):
            snap(f"h{h}")
            emit("vector", lambda E: E.memset(Vt[:, :], 1.0), writes=("Vt",))
            for r in range(4):
                for tt in range(4):
                    for c in range(NC_D):
                        load(xa[:, c * 512:(c + 1) * 512],
                             dxng[r * D + c * 128: r * D + (c + 1) * 128,
                                  tt * 512:(tt + 1) * 512], f"xa{c}",
                             reads=("dxng",))
                    for c in range(NC_D):
                        mm(PK[2], PS[2][0:64, 0:512],
                           qw[:, c * 768 + 256 + h * 64:
                              c * 768 + 256 + (h + 1) * 64],
                           xa[:, c * 512:(c + 1) * 512],
                           (f"qw{c}", f"xa{c}"), c == 0, c == NC_D - 1)
                    emit("vector", lambda E, h=h, r=r, tt=tt: E.tensor_scalar(
                        Kh[:, r * 2048 + tt * 512: r * 2048 + (tt + 1) * 512],
                        PS[2][0:64, 0:512], bqT[:, 4 + h:5 + h], None,
                        ALU.add),
                        reads=(PK[2], "bqT"), writes=("Kh",))
                    for c in range(NC_D):
                        mm(PK[3], PS[3][0:64, 0:512],
                           qw[:, c * 768 + 512 + h * 64:
                              c * 768 + 512 + (h + 1) * 64],
                           xa[:, c * 512:(c + 1) * 512],
                           (f"qw{c}", f"xa{c}"), c == 0, c == NC_D - 1)
                    emit("vector", lambda E, h=h, r=r, tt=tt: E.tensor_scalar(
                        Vf[:, r * 2048 + tt * 512: r * 2048 + (tt + 1) * 512],
                        PS[3][0:64, 0:512], bqT[:, 8 + h:9 + h], None,
                        ALU.add),
                        reads=(PK[3], "bqT"), writes=("Vf",))
            for kt in range(64):
                transp(PBK[kt % 2], PB[kt % 2][0:128, 0:64],
                       Vf[:, kt * 128:(kt + 1) * 128], id_b[0:64, 0:64],
                       ("Vf", "idb"))
                emit("vector", lambda E, kt=kt: E.tensor_copy(
                    Vt[:, kt * 65:kt * 65 + 64], PB[kt % 2][:, 0:64]),
                    reads=(PBK[kt % 2],), writes=("Vt",))
            for qc in range(16):
                po = 4 + (qc % 2)
                load(qld[qc % 2][:, :],
                     dq[:, h * SFULL + qc * 512: h * SFULL + (qc + 1) * 512],
                     f"qld{qc % 2}", reads=("dq_st",))
                for g_ in range(16):
                    gi = g_ % 4
                    for i4 in range(4):
                        mm(PK[i4], PS[i4][0:128, 0:512],
                           Kh[:, (g_ * 4 + i4) * 128:(g_ * 4 + i4 + 1) * 128],
                           qld[qc % 2][:, :],
                           ("Kh", f"qld{qc % 2}"), True, True)
                    emit("scalar", lambda E, gi=gi: E.activation(
                        Pt[gi][:, :], PSW[:, :], AF.Exp, scale=0.125),
                        reads=(PK[0], PK[1], PK[2], PK[3]),
                        writes=(f"Pt{gi}",))
                    for i4 in range(4):
                        kt = g_ * 4 + i4
                        mm(PK[po], PS[po][0:65, 0:512],
                           Vt[:, kt * 65:(kt + 1) * 65],
                           Pt[gi][:, i4 * 512:(i4 + 1) * 512],
                           ("Vt", f"Pt{gi}"), kt == 0, kt == 63)
                emit("vector", lambda E, po=po, qc=qc: E.tensor_copy(
                    sO[qc % 2][:, :], PS[po][0:65, 0:512]),
                    reads=(PK[po],), writes=(f"sO{qc % 2}",))
                for v in range(4):
                    transp(PK[2], PS[2][0:128, 0:65],
                           sO[qc % 2][:, v * 128:(v + 1) * 128],
                           id_f[0:65, 0:65], (f"sO{qc % 2}", "idf"))
                    emit("vector", lambda E: E.reciprocal(
                        rz_[:, :], PS[2][:, 64:65]),
                        reads=(PK[2],), writes=("rz",))
                    qb = qc * 4 + v
                    emit("vector", lambda E, qb=qb, h=h: E.tensor_scalar(
                        aac[:, qb * 256 + h * 64: qb * 256 + (h + 1) * 64],
                        PS[2][:, 0:64], rz_[:, :], None, ALU.mult),
                        reads=(PK[2], "rz"), writes=(f"aac{qb}_{h}",))

        snap("att")
        for qb in range(64):
            for ff in range(2):
                transp(PBK[ff], PB[ff][0:128, 0:128],
                       aac[:, qb * 256 + ff * 128: qb * 256 + (ff + 1) * 128],
                       id_b[:, :],
                       tuple(f"aac{qb}_{h}" for h in range(NH)) + ("idb",))
                emit("vector", lambda E, ff=ff: E.tensor_copy(
                    aaT[:, ff * 128:(ff + 1) * 128], PB[ff][:, 0:128]),
                    reads=(PBK[ff],), writes=("aaT",))
            for half in range(2):
                pi = half
                for ff in range(2):
                    mm(PK[pi], PS[pi][0:128, 0:512],
                       aaT[:, ff * 128:(ff + 1) * 128],
                       plcs[:, ff * D + half * 512: ff * D + (half + 1) * 512],
                       ("aaT", "plcs"), ff == 0, ff == 1)
                emit("vector", lambda E, pi=pi, half=half: E.tensor_copy(
                    afull[:, half * 512:(half + 1) * 512], PS[pi][:, 0:512]),
                    reads=(PK[pi],), writes=("afull",))
            store(dati[qb * 128:(qb + 1) * 128, :], afull[:, :], ("afull",),
                  "dati_st")
        emit("gpsimd", lambda E: E.collective_compute(
            "ReduceScatter", ALU.add, replica_groups=RG,
            ins=[dati.opt()], outs=[dato.opt()]),
            reads=("dati_st",), writes=("dato",), cc="cc3")

        snap("c")
        barrier()
        cs.close()

        b2 = contextlib.ExitStack()
        gm_ = [sb(b2, f"gm{i}", [128, NC_D * 128], BF16) for i in range(2)]
        um_ = [sb(b2, f"um{i}", [128, NC_D * 128], BF16) for i in range(2)]
        dm_ = [sb(b2, f"dm{i}", [128, 32 * 128], BF16) for i in range(2)]
        sg_ = sb(b2, "sg_", [128, 512], BF16)
        su_ = sb(b2, "su_", [128, 512], BF16)
        Vbig = sb(b2, "Vbig", [128, 32 * 512], BF16)
        hrl = [sb(b2, f"hrl{i}", [128, 512]) for i in range(2)]
        sst = sb(b2, "sst", [128, 512])
        stok = sb(b2, "stok", [128, 4 * 1024])

        for tb in range(4):
            for m in range(32):
                load(gm_[m % 2][:, :].rearrange("p (c f) -> p c f", c=NC_D),
                     dgw2.rearrange("p (c f) -> p c f", c=NC_D)[
                         :, :, m * 128:(m + 1) * 128],
                     f"gm{m % 2}", reads=("dgw2_st",))
                load(um_[m % 2][:, :].rearrange("p (c f) -> p c f", c=NC_D),
                     duw2.rearrange("p (c f) -> p c f", c=NC_D)[
                         :, :, m * 128:(m + 1) * 128],
                     f"um{m % 2}", reads=("duw2_st",))
                for c in range(NC_D):
                    mm(PK[0], PS[0][0:128, 0:512],
                       gm_[m % 2][:, c * 128:(c + 1) * 128],
                       hn[:, c * T + tb * 512: c * T + (tb + 1) * 512],
                       (f"gm{m % 2}", f"hn{c}_{tb}"), c == 0, c == NC_D - 1)
                emit("scalar", lambda E: E.activation(
                    sg_[:, :], PS[0][:, 0:512], AF.Silu),
                    reads=(PK[0],), writes=("sg",))
                for c in range(NC_D):
                    mm(PK[1], PS[1][0:128, 0:512],
                       um_[m % 2][:, c * 128:(c + 1) * 128],
                       hn[:, c * T + tb * 512: c * T + (tb + 1) * 512],
                       (f"um{m % 2}", f"hn{c}_{tb}"), c == 0, c == NC_D - 1)
                emit("vector", lambda E: E.tensor_copy(su_[:, :],
                                                       PS[1][:, 0:512]),
                     reads=(PK[1],), writes=("su",))
                emit("gpsimd", lambda E, m=m: E.tensor_tensor(
                    Vbig[:, m * 512:(m + 1) * 512], sg_[:, :], su_[:, :],
                    ALU.mult),
                    reads=("sg", "su"), writes=(f"Vbig{m}",))
            for ot in range(NC_D):
                load(dm_[ot % 2][:, :].rearrange("p (m f) -> p m f", m=32),
                     ddw2.rearrange("p (m f) -> p m f", m=32)[
                         :, :, ot * 128:(ot + 1) * 128],
                     f"dm{ot % 2}", reads=("ddw2_st",))
                pi = 4 + ot % 2
                for m in range(32):
                    mm(PK[pi], PS[pi][0:128, 0:512],
                       dm_[ot % 2][:, m * 128:(m + 1) * 128],
                       Vbig[:, m * 512:(m + 1) * 512],
                       (f"dm{ot % 2}", f"Vbig{m}"), m == 0, m == 31)
                load(hrl[ot % 2][:, :],
                     dhid[ot * 128:(ot + 1) * 128, tb * 512:(tb + 1) * 512],
                     f"hrl{ot % 2}", reads=("dhid_st",))
                emit("vector", lambda E, pi=pi, ot=ot: E.tensor_tensor(
                    sst[:, :], PS[pi][:, 0:512], hrl[ot % 2][:, :], ALU.add),
                    reads=(PK[pi], f"hrl{ot % 2}"), writes=("sst",))
                for v in range(4):
                    transp(PK[2], PS[2][0:128, 0:128],
                           sst[:, v * 128:(v + 1) * 128], id_f[:, :],
                           ("sst", "idf"))
                    emit("vector", lambda E, v=v, ot=ot: E.tensor_copy(
                        stok[:, v * 1024 + ot * 128:
                             v * 1024 + (ot + 1) * 128],
                        PS[2][:, 0:128]),
                        reads=(PK[2],), writes=(f"stok{v}_{ot}",))
            for v in range(4):
                tt = tb * 4 + v
                store(dsm[tt * 128:(tt + 1) * 128, :],
                      stok[:, v * 1024:(v + 1) * 1024],
                      tuple(f"stok{v}_{ot}" for ot in range(NC_D)), "dsm_st")

        snap("b2")
        barrier()
        b2.close()
        hnst.close()

        fs_ = contextlib.ExitStack()
        xf = [sb(fs_, f"xf{i}", [128, D]) for i in range(2)]
        sf = [sb(fs_, f"sf{i}", [128, D]) for i in range(2)]
        mf = [sb(fs_, f"mf{i}", [128, D]) for i in range(2)]
        af = [sb(fs_, f"af{i}", [128, D], BF16) for i in range(2)]
        gf = sb(fs_, "gf", [128, D])
        of = [sb(fs_, f"of{i}", [128, D]) for i in range(2)]

        for tt in range(16):
            i = tt % 2
            load(xf[i][:, :], x_own[tt * 128:(tt + 1) * 128, :], f"xf{i}")
            load(sf[i][:, :], dsm[tt * 128:(tt + 1) * 128, :], f"sf{i}",
                 reads=("dsm_st",))
            load(mf[i][:, :], dmoe[tt * 128:(tt + 1) * 128, :], f"mf{i}",
                 reads=("dmoe_st",))
            load(af[i][:, :], dato[tt * 128:(tt + 1) * 128, :], f"af{i}",
                 reads=("dato",))
            emit("vector", lambda E, i=i: E.tensor_tensor(
                gf[:, :], mf[i][:, :], af[i][:, :], ALU.add),
                reads=(f"mf{i}", f"af{i}"), writes=("gf",))
            emit("scalar", lambda E: E.activation(gf[:, :], gf[:, :], AF.Silu),
                 reads=("gf",), writes=("gf",))
            emit("vector", lambda E, i=i: E.tensor_tensor(
                gf[:, :], gf[:, :], sf[i][:, :], ALU.mult),
                reads=("gf", f"sf{i}"), writes=("gf",))
            emit("vector", lambda E, i=i: E.tensor_tensor(
                of[i][:, :], xf[i][:, :], gf[:, :], ALU.add),
                reads=("gf", f"xf{i}"), writes=(f"of{i}",))
            store(out_t[tt * 128:(tt + 1) * 128, :], of[i][:, :],
                  (f"of{i}",), f"out{i}")

        for i in range(8):
            _wait("sync", f"q_scalar_{i}", cnt[f"q_scalar_{i}"])
        fs_.close()

        if cut is not None:
            lim = snaps[cut]
            for e in ENGS:
                del q[e][lim[e]:]
        block = es.enter_context(nc.Block())
        Eh = {}

        @block.sync
        def _(sync):
            Eh["sync"] = sync
            for fn in q["sync"]:
                fn(Eh)

        @block.tensor
        def _(tensor):
            Eh["tensor"] = tensor
            for fn in q["tensor"]:
                fn(Eh)

        @block.vector
        def _(vector):
            Eh["vector"] = vector
            for fn in q["vector"]:
                fn(Eh)

        @block.scalar
        def _(scalar):
            Eh["scalar"] = scalar
            for fn in q["scalar"]:
                fn(Eh)

        @block.gpsimd
        def _(gpsimd):
            Eh["gpsimd"] = gpsimd
            for fn in q["gpsimd"]:
                fn(Eh)

    return nc


def kernel(**inp):
    x = inp["L0_attn_hidden_states"]
    B, S, _ = x.shape
    bf = ml_dtypes.bfloat16

    idf_h = np.eye(128, dtype=np.float32)
    sel_h = np.zeros((NST, NST * 128), dtype=np.float32)
    for n in range(NST):
        sel_h[n, n * 128:(n + 1) * 128] = 1.0

    wq = inp["L0_ffn_W_qkv"].reshape(D, 3, 16, 64)
    bq = inp["L0_ffn_b_qkv"].reshape(3, 16, 64)

    in_maps = []
    for c in range(8):
        b, j = c // 4, c % 4
        cwb_h = np.zeros((128, 16), dtype=np.float32)
        cmb_h = np.zeros((128, 4), dtype=np.float32)
        for r in range(4):
            if r < j:
                cmb_h[:, r] = 1.0
            for m_ in range(4):
                if r < m_ < j:
                    cwb_h[:, r * 4 + m_] = 1.0
        halo = (np.zeros((3, D), dtype=np.float32) if j == 0
                else np.ascontiguousarray(x[b, j * 2048 - 3: j * 2048]))
        plc_h = np.zeros((256, D), dtype=np.float32)
        for f in range(256):
            plc_h[f, j * 256 + f] = 1.0
        m = {
            "x_own": np.ascontiguousarray(x[b, j * 2048:(j + 1) * 2048]),
            "x_halo": halo,
            "wip": inp["L0_attn_in_proj_w"],
            "convw": inp["L0_attn_conv_w"],
            "convb": inp["L0_attn_conv_b"],
            "xpw": inp["L0_attn_x_proj_w"],
            "dtw": inp["L0_attn_dt_proj_w"],
            "dtb": inp["L0_attn_dt_proj_b"],
            "dtln": inp["L0_attn_dt_ln_w"],
            "bln": inp["L0_attn_b_ln_w"],
            "cln": inp["L0_attn_c_ln_w"],
            "alog": inp["L0_attn_A_log"],
            "dskip": inp["L0_attn_D_skip"],
            "opw": inp["L0_attn_out_proj_w"],
            "ilnw": inp["L0_attn_input_ln_w"],
            "plnw": inp["L0_attn_pre_moe_ln_w"],
            "gw": inp["L0_attn_gate_w"],
            "uw": inp["L0_attn_up_w"],
            "dwn": inp["L0_attn_down_w"],
            "qkvw": np.ascontiguousarray(
                wq[:, :, 4 * j:4 * j + 4, :].reshape(D, 768)),
            "qkvb": np.ascontiguousarray(
                bq[:, 4 * j:4 * j + 4, :].reshape(12, 64)),
            "wgate": inp["L0_ffn_W_gate"],
            "bgate": inp["L0_ffn_b_gate"].reshape(1, 8),
            "wexp": inp["L0_ffn_W_experts"],
            "bexp": inp["L0_ffn_b_experts"],
            "idf": idf_h,
            "selone": sel_h,
            "cwb": cwb_h,
            "cmb": cmb_h,
            "plc": plc_h,
        }
        in_maps.append(m)

    import os
    nc = bld(cut=os.environ.get("KCUT") or None)
    res = run_bass_kernel_spmd(nc, in_maps, core_ids=list(range(8)))
    out = np.empty((B, S, D), dtype=np.float32)
    for c in range(8):
        b, j = c // 4, c % 4
        out[b, j * 2048:(j + 1) * 2048] = res.results[c]["out"]
    return out


